# version 07 (final)
# baseline (speedup 1.0000x reference)
import contextlib

import numpy as np

import concourse.bass as bass
import concourse.mybir as mybir
from concourse.bass_utils import run_bass_kernel_spmd

F32 = mybir.dt.float32
F32R = mybir.dt.float32r
BF16 = mybir.dt.bfloat16
ALU = mybir.AluOpType
AF = mybir.ActivationFunctionType
AX = mybir.AxisListType

D = 1024
NST = 16
E = 16
HID = 2048
NCORES = 8
BPC = 2
T = 2048
TOK = BPC * T
TP = 1024
NCH = 2
TC = 512
DG = 8
HCN = 16
SUB = 2
R1 = 4
R2 = 4
LNEPS = 1e-5
GEPS = 1e-12
NCONST = 13


def build_schedule(NP):
    sch = []
    for j in range(16):
        sch.append(("b1i", j))
    for j in range(16):
        sch.append(("wbi", j))

    def mixer_units(p, c):
        units = []
        for q in range(4):
            units.append([("xin", p, c, q)] +
                         [("xp", p, c, q, dg) for dg in range(DG)])
        units.append([("bmt", p, c)])
        units.append([("cmt", p, c)])
        units.append([("zt", p, c)])
        units.append([("drep", p, c)])
        for n in range(NST):
            units.append([("scan", p, c, n)])
        units.append([("sx", p, c), ("sxsq", p, c),
                      ("mrep", p, c), ("irep", p, c)])
        units.append([("x1st", p, c)])
        return units

    for u in mixer_units(0, 0) + mixer_units(0, 1):
        sch.extend(u)

    uq = []
    for p in range(NP):
        sch.append(("reload", p))
        if p >= 1:
            for tg in range(8):
                for dg in range(DG):
                    sch.append(("otr", p - 1, tg, dg))
                sch.append(("ln2", p - 1, tg))
        for tt in range(8):
            sch.append(("log", p, tt))
        for j in range(8):
            sch.append(("gtr", p, j))
        for dg in range(DG):
            for sb in range(SUB):
                sch.append(("b2", p, dg, sb))
        if p + 1 < NP:
            for u in mixer_units(p + 1, 0) + mixer_units(p + 1, 1):
                uq.append((p + 1, u))

        UCOST = {"xin": 5, "bmt": 1, "cmt": 1, "zt": 3, "drep": 1,
                 "scan": 30, "sx": 12, "x1st": 1}
        total = sum(UCOST[u[0][0]] for _, u in uq)
        tk = [0]
        SKIP = 12
        WTOT = E * 10 - SKIP

        def take_to(w):
            target = total * max(0, w - SKIP) // WTOT
            while uq and tk[0] < target:
                _, u = uq.pop(0)
                sch.extend(u)
                tk[0] += UCOST[u[0][0]]
        for e in range(E):
            sch.append(("grep", p, e, 0))
            sch.append(("grep", p, e, 1))
            sch.append(("ph1", p, e, 0))
            take_to(e * 10 + 4)
            sch.append(("ph1", p, e, 1))
            take_to(e * 10 + 8)
            sch.append(("ph2", p, e, 0))
            take_to(e * 10 + 9)
            sch.append(("ph2", p, e, 1))
            take_to(e * 10 + 10)
        while uq and uq[0][0] <= p + 1:
            _, u = uq.pop(0)
            sch.extend(u)
    for tg in range(8):
        for dg in range(DG):
            sch.append(("otr", NP - 1, tg, dg))
        sch.append(("ln2", NP - 1, tg))
    return sch


PMM_KINDS = {"b1i", "wbi", "xp", "bmt", "cmt", "zt", "scan",
             "sx", "sxsq", "mrep", "irep", "log", "gtr", "b2", "grep", "otr"}


def build_program(NP=4, race_detect=True, taps=False):
    sch = build_schedule(NP)
    PMM = {}
    ctr = 0
    for item in sch:
        if item[0] in PMM_KINDS:
            PMM[item] = ctr
            ctr += 2 if item[0] == "scan" else 1
    nc = bass.Bass("TRN2", target_bir_lowering=False,
                   detect_race_conditions=race_detect)

    x_d = nc.dram_tensor("x", [TOK, D], F32R, kind="ExternalInput").ap()
    a_d = nc.dram_tensor("amat", [D, NST], F32, kind="ExternalInput").ap()
    wb_d = nc.dram_tensor("wb", [NST, D], F32R, kind="ExternalInput").ap()
    bb_d = nc.dram_tensor("bb", [NST, 1], F32, kind="ExternalInput").ap()
    wc_d = nc.dram_tensor("wc", [NST, D], F32R, kind="ExternalInput").ap()
    bc_d = nc.dram_tensor("bc", [NST, 1], F32, kind="ExternalInput").ap()
    wd_d = nc.dram_tensor("wd", [D, 1], F32R, kind="ExternalInput").ap()
    bd_d = nc.dram_tensor("bd", [1, 1], F32, kind="ExternalInput").ap()
    biasd_d = nc.dram_tensor("biasd", [1, 1], F32, kind="ExternalInput").ap()
    gw_d = nc.dram_tensor("gw", [D, E], F32R, kind="ExternalInput").ap()
    w1_d = nc.dram_tensor("w1", [E, D, HID], F32R, kind="ExternalInput").ap()
    b1_d = nc.dram_tensor("b1", [E, HID], F32, kind="ExternalInput").ap()
    w2_d = nc.dram_tensor("w2", [E, HID, D], F32, kind="ExternalInput").ap()
    b2_d = nc.dram_tensor("b2", [E, D], F32, kind="ExternalInput").ap()
    eyer_d = nc.dram_tensor("eyer", [128, 128], F32R, kind="ExternalInput").ap()
    eyef_d = nc.dram_tensor("eyef", [128, 128], F32, kind="ExternalInput").ap()
    onescol_d = nc.dram_tensor("onescol", [128, 1], F32R, kind="ExternalInput").ap()
    onesrow_d = nc.dram_tensor("onesrow", [1, 128], F32R, kind="ExternalInput").ap()
    onehot_d = nc.dram_tensor("onehot", [E, E * 128], BF16, kind="ExternalInput").ap()
    y_d = nc.dram_tensor("y", [TOK, D], F32, kind="ExternalOutput").ap()
    x1scr_d = nc.dram_tensor("x1scr", [NP * 128, DG * TP], F32R, kind="Internal").ap()
    if taps:
        tap_d = {
            "t_x1": nc.dram_tensor("t_x1", [128, DG * TP], F32, kind="ExternalOutput").ap(),
            "t_ya": nc.dram_tensor("t_ya", [128, DG * TP], F32, kind="ExternalOutput").ap(),
            "t_gt": nc.dram_tensor("t_gt", [NST, TP], F32, kind="ExternalOutput").ap(),
        }

    ctx = contextlib.ExitStack()
    EE = ctx.enter_context

    sW1 = EE(nc.sbuf_tensor("sW1", [128, R1 * DG * 128], BF16))
    sW2 = EE(nc.sbuf_tensor("sW2", [128, R2 * HCN * 128], BF16))
    sH = EE(nc.sbuf_tensor("sH", [128, HCN * TP], BF16))
    sX1b = EE(nc.sbuf_tensor("sX1b", [128, DG * TP], BF16))
    sYM = EE(nc.sbuf_tensor("sYM", [128, DG * TC], F32R))
    sYA = EE(nc.sbuf_tensor("sYA", [128, DG * TP], F32))
    sXT = EE(nc.sbuf_tensor("sXT", [128, DG * TC], F32R))
    sInX = EE(nc.sbuf_tensor("sInX", [128, D], F32R))
    sOutT = EE(nc.sbuf_tensor("sOutT", [128, 2 * D], BF16))
    sAnat = EE(nc.sbuf_tensor("sAnat", [128, 128], F32))
    sInvA = EE(nc.sbuf_tensor("sInvA", [128, 128], F32))
    sCarry = EE(nc.sbuf_tensor("sCarry", [128, 128], F32))
    sAbar = EE(nc.sbuf_tensor("sAbar", [128, 4 * TC], BF16))
    sT1 = EE(nc.sbuf_tensor("sT1", [128, TC], F32R))
    sT2 = EE(nc.sbuf_tensor("sT2", [128, TC], F32R))
    sT3 = EE(nc.sbuf_tensor("sT3", [128, TC], F32))
    sWBc = EE(nc.sbuf_tensor("sWBc", [128, 128], F32R))
    sWCc = EE(nc.sbuf_tensor("sWCc", [128, 128], F32R))
    sWDc = EE(nc.sbuf_tensor("sWDc", [128, 8], F32R))
    sGW = EE(nc.sbuf_tensor("sGW", [128, 128], BF16))
    sB1c = EE(nc.sbuf_tensor("sB1c", [128, HCN * E], F32))
    sB2 = EE(nc.sbuf_tensor("sB2", [E, D], BF16))
    sGateT = EE(nc.sbuf_tensor("sGateT", [E, TP], BF16))
    sGRep = EE(nc.sbuf_tensor("sGRep", [128, TP], BF16))
    sGfin = EE(nc.sbuf_tensor("sGfin", [128, 128], F32))
    sGw1 = EE(nc.sbuf_tensor("sGw1", [128, 128], F32))
    sGw2 = EE(nc.sbuf_tensor("sGw2", [128, 128], F32))
    sGw3 = EE(nc.sbuf_tensor("sGw3", [128, 128], F32))
    sGcol = EE(nc.sbuf_tensor("sGcol", [128, 64], F32))
    sBmT = EE(nc.sbuf_tensor("sBmT", [NST, TC], BF16))
    sBrep = EE(nc.sbuf_tensor("sBrep", [128, 2 * TC], BF16))
    sCrep = EE(nc.sbuf_tensor("sCrep", [128, 2 * TC], BF16))
    sCmT = EE(nc.sbuf_tensor("sCmT", [NST, TC], BF16))
    sRowF = EE(nc.sbuf_tensor("sRowF", [1, 3 * TC], F32))
    sRowR = EE(nc.sbuf_tensor("sRowR", [1, 2 * TC], F32R))
    sBB = EE(nc.sbuf_tensor("sBB", [NST, 1], F32))
    sBC = EE(nc.sbuf_tensor("sBC", [NST, 1], F32))
    sBD = EE(nc.sbuf_tensor("sBD", [1, 2], F32))
    sL2c = EE(nc.sbuf_tensor("sL2c", [128, 8], F32))
    sEyeR = EE(nc.sbuf_tensor("sEyeR", [128, 128], F32R))
    sOnesC = EE(nc.sbuf_tensor("sOnesC", [128, 1], F32R))
    sOnesR = EE(nc.sbuf_tensor("sOnesR", [1, 128], F32R))
    sOneH = EE(nc.sbuf_tensor("sOneH", [E, E * 128], BF16))
    sDum = EE(nc.sbuf_tensor("sDum", [1, 8], F32))

    b1nat = sYA.ap()[0:16, 0:HID]
    wbnat = sXT.ap()[0:16, 0:1024]
    wcnat = sXT.ap()[0:16, 1024:2048]
    t1 = sT1.ap().bitcast(F32)
    t2 = sT2.ap().bitcast(F32)
    t1w = sT1.ap()
    t2w = sT2.ap()
    tsq = [sT1.ap(), sT2.ap()]

    pH = EE(nc.psum_tensor("pH", [128, 1024], F32))
    pY = EE(nc.psum_tensor("pY", [128, 1024], F32))
    pMM = EE(nc.psum_tensor("pMM", [128, 1536], F32))
    pDR = EE(nc.psum_tensor("pDR", [128, 512], F32))
    pMM_r = pMM.ap().bitcast(F32R)
    sEyeF = sEyeR.ap().bitcast(F32)
    shx1 = sH.ap().bitcast(F32)

    semnames = [
        "sConst", "sXq", "sW1l", "fW1", "sW2l", "fW2",
        "sPmm", "vFreePmm", "vZrow", "aSp", "vDrow",
        "vMix", "vSq", "pSq", "vStat", "aStat", "aAbar", "vAbar",
        "pHgrp", "aFreeH", "pYgrp", "vFreeY", "pX1done",
        "vGexp", "aGexp", "vGfin", "vGateT", "vLn1", "vLn2", "aLn2",
        "vOut", "st", "vv", "pDRs", "aFreeDR", "x1s", "sX1L", "sB2l", "sX1bL",
    ]
    S = {n: EE(nc.semaphore(n)) for n in semnames}
    block = EE(nc.Block())

    def ch(p, c):
        return p * NCH + c

    def mmslot(k):
        return (k % 3) * 512

    @block.sync
    def _(sync):
        sync.dma_start(out=sEyeR[:, :], in_=eyer_d).then_inc(S["sConst"], 16)
        sync.dma_start(out=sOnesC[:, :], in_=onescol_d).then_inc(S["sConst"], 16)
        sync.dma_start(out=sOnesR[:, :], in_=onesrow_d).then_inc(S["sConst"], 16)
        sync.dma_start(out=sOneH[:, :], in_=onehot_d).then_inc(S["sConst"], 16)
        sync.dma_start(
            out=sAnat[:, :].rearrange("p (dg n) -> p dg n", dg=DG),
            in_=a_d.rearrange("(dg p) n -> p dg n", p=128),
        ).then_inc(S["sConst"], 16)
        sync.dma_start(out=wbnat, in_=wb_d).then_inc(S["sConst"], 16)
        sync.dma_start(out=wcnat, in_=wc_d).then_inc(S["sConst"], 16)
        sync.dma_start(
            out=sWDc[:, :].rearrange("p (dg o) -> p dg o", dg=DG),
            in_=wd_d.rearrange("(dg p) o -> p dg o", p=128),
            allow_slow_non_contiguous=True,
        ).then_inc(S["sConst"], 16)
        sync.dma_start(out=sBB[:, :], in_=bb_d).then_inc(S["sConst"], 16)
        sync.dma_start(out=sBC[:, :], in_=bc_d).then_inc(S["sConst"], 16)
        sync.dma_start(out=sBD[:, 0:1], in_=bd_d).then_inc(S["sConst"], 16)
        sync.dma_start(out=sBD[:, 1:2], in_=biasd_d).then_inc(S["sConst"], 16)
        sync.dma_start(out=b1nat, in_=b1_d).then_inc(S["sConst"], 16)

        x1f = x1scr_d.bitcast(F32)
        for p in range(NP):
            if p >= 1:
                sync.wait_ge(S["pYgrp"], E * DG * SUB * p)
            sync.wait_ge(S["x1s"], 16 * (2 * p + 2))
            sync.dma_start(
                out=shx1[:, :], in_=x1f[p * 128:(p + 1) * 128, :]
            ).then_inc(S["sX1L"], 16)
        sync.wait_ge(S["st"], 16 * 8 * NP)

    @block.gpsimd
    def _(gpsimd):
        gpsimd.dma_start(out=sB2[:, :], in_=b2_d).then_inc(S["sB2l"], 16)
        gpsimd.dma_start(
            out=sGW[:, :].rearrange("p (dg e) -> p dg e", dg=DG),
            in_=gw_d.bitcast(F32).rearrange("(dg p) e -> p dg e", p=128),
        ).then_inc(S["sB2l"], 16)
        w1f = w1_d.bitcast(F32)

        def w1col(k):
            if k >= R1:
                gpsimd.wait_ge(S["fW1"], k - R1 + 1)
            slot = k % R1
            e, hc = divmod(k % (E * HCN), HCN)
            gpsimd.dma_start(
                out=sW1[:, slot * 1024:(slot + 1) * 1024].rearrange(
                    "p (dg h) -> p dg h", dg=DG),
                in_=w1f[e, :, hc * 128:(hc + 1) * 128].rearrange(
                    "(dg p) h -> p dg h", p=128),
            ).then_inc(S["sW1l"], 16)

        def w2col(k):
            if k >= R2:
                gpsimd.wait_ge(S["fW2"], k - R2 + 1)
            slot = k % R2
            e, dg = divmod(k % (E * DG), DG)
            p = k // (E * DG)
            gpsimd.dma_start(
                out=sW2[:, slot * 2048:(slot + 1) * 2048].rearrange(
                    "p (hc d) -> p hc d", hc=HCN),
                in_=w2_d[e, :, dg * 128:(dg + 1) * 128].rearrange(
                    "(hc p) d -> p hc d", p=128),
            ).then_inc(S["sW2l"], 16)

        for p in range(NP):
            base = p * E * DG
            w1base = p * E * HCN
            for k in range(w1base, w1base + R1):
                w1col(k)
            if p >= 1:
                for tg in range(8):
                    g = (p - 1) * 8 + tg
                    gpsimd.wait_ge(S["vOut"], g + 1)
                    row = (p - 1) * TP + tg * 128
                    gpsimd.dma_start(
                        out=y_d[row:row + 128, :],
                        in_=sOutT[:, (g % 2) * D:(g % 2 + 1) * D]
                    ).then_inc(S["st"], 16)
            if p >= 1:
                gpsimd.wait_ge(S["pYgrp"], E * DG * SUB * p)
            gpsimd.wait_ge(S["x1s"], 16 * (2 * p + 2))
            gpsimd.dma_start(
                out=sX1b[:, :],
                in_=x1scr_d.bitcast(F32)[p * 128:(p + 1) * 128, :]
            ).then_inc(S["sX1bL"], 16)
            for e in range(E):
                lo = w1base + e * HCN
                for k in range(max(lo, w1base + R1), lo + HCN):
                    w1col(k)
                for k in range(base + e * DG, base + (e + 1) * DG):
                    w2col(k)
        for tg in range(8):
            g = (NP - 1) * 8 + tg
            gpsimd.wait_ge(S["vOut"], g + 1)
            row = (NP - 1) * TP + tg * 128
            gpsimd.dma_start(
                out=y_d[row:row + 128, :],
                in_=sOutT[:, (g % 2) * D:(g % 2 + 1) * D]
            ).then_inc(S["st"], 16)
        if taps:
            gpsimd.wait_ge(S["st"], 16 * 8 * NP)
            gpsimd.dma_start(out=tap_d["t_x1"], in_=sX1b[:, :]).then_inc(S["st"], 16)
            gpsimd.dma_start(out=tap_d["t_ya"], in_=sYA[:, :]).then_inc(S["st"], 16)
            gpsimd.dma_start(out=tap_d["t_gt"], in_=sGateT.ap().bitcast(F32)).then_inc(S["st"], 16)
            gpsimd.wait_ge(S["st"], 16 * 8 * NP + 48)

    @block.tensor
    def _(tensor):
        extra = []

        def grp(key, n_mm, mk_mm, pre=None):
            k = PMM[key]
            if k >= 3:
                tensor.wait_ge(S["vFreePmm"], k - 2)
            if pre:
                pre()
            half = mmslot(k)
            for i in range(n_mm):
                ins = mk_mm(i, half, i == 0, i == n_mm - 1)
                if i == n_mm - 1:
                    ins.then_inc(S["sPmm"], 1)
                elif extra:
                    sem, val = extra.pop(0)
                    ins.then_inc(sem, val)

        tensor.wait_ge(S["sConst"], 16 * NCONST)
        for item in sch:
            kind = item[0]
            if kind == "b1i":
                j = item[1]
                grp(item, 1, lambda i, half, st, sp, j=j: tensor.transpose(
                    pMM[:, half:half + 16],
                    b1nat[:, j * 128:(j + 1) * 128], sEyeF[0:16, 0:16]))
            elif kind == "wbi":
                j = item[1]
                src = wbnat if j < 8 else wcnat
                dgj = j % 8
                grp(item, 1, lambda i, half, st, sp, src=src, dgj=dgj:
                    tensor.transpose(
                        pMM_r[:, half:half + 16],
                        src[:, dgj * 128:(dgj + 1) * 128], sEyeR[0:16, 0:16]))
            elif kind == "xp":
                _, p, c, q, dg = item
                if dg == 0:
                    Q = ch(p, c) * 4 + q
                    tensor.wait_ge(S["sXq"], 16 * (Q + 1))
                grp(item, 1, lambda i, half, st, sp, dg=dg: tensor.transpose(
                    pMM_r[:, half:half + 128],
                    sInX[:, dg * 128:(dg + 1) * 128], sEyeR[:, :]))
            elif kind in ("bmt", "cmt"):
                _, p, c = item
                w = sWBc if kind == "bmt" else sWCc
                pre = None
                if kind == "bmt":
                    pre = lambda p=p, c=c: tensor.wait_ge(
                        S["vFreePmm"], PMM[("xp", p, c, 3, DG - 1)] + 1)
                grp(item, DG, lambda i, half, st, sp, w=w: tensor.matmul(
                    pMM[0:16, half:half + TC], w[:, i * 16:(i + 1) * 16],
                    sXT[:, i * TC:(i + 1) * TC], start=st, stop=sp), pre=pre)
            elif kind == "zt":
                grp(item, DG, lambda i, half, st, sp: tensor.matmul(
                    pMM[0:1, half:half + TC], sWDc[:, i:i + 1],
                    sXT[:, i * TC:(i + 1) * TC], start=st, stop=sp))
            elif kind == "drep":
                _, p, c = item
                cn = ch(p, c)
                if cn >= 1:
                    tensor.wait_ge(S["aFreeDR"], cn)
                tensor.wait_ge(S["vDrow"], cn + 1)
                tensor.matmul(
                    pDR[:, 0:TC], sOnesR[:, :], sRowR[0:1, 0:TC],
                    start=True, stop=True,
                ).then_inc(S["pDRs"], 1)
            elif kind == "scan":
                _, p, c, n = item
                k = PMM[item]
                for gi, src in enumerate((sBmT, sCmT)):
                    kk = k + gi
                    if kk >= 3:
                        tensor.wait_ge(S["vFreePmm"], kk - 2)
                    half = mmslot(kk)
                    tensor.matmul(
                        pMM[:, half:half + TC],
                        sOneH[:, n * 128:(n + 1) * 128], src[:, :],
                        start=True, stop=True,
                    ).then_inc(S["sPmm"], 1)
            elif kind == "sx":
                _, p, c = item
                cn = ch(p, c)
                k = PMM[item]
                tensor.wait_ge(S["vFreePmm"], k - 2)
                half = mmslot(k)
                for dg in range(DG):
                    tensor.wait_ge(S["vMix"], cn * 8 + dg + 1)
                    ins = tensor.matmul(
                        pMM[0:1, half:half + TC], sOnesC[:, 0:1],
                        sYM[:, dg * TC:(dg + 1) * TC],
                        start=(dg == 0), stop=(dg == DG - 1))
                    if dg == DG - 1:
                        ins.then_inc(S["sPmm"], 1)
                    elif extra:
                        sem, val = extra.pop(0)
                        ins.then_inc(sem, val)
            elif kind == "sxsq":
                _, p, c = item
                cn = ch(p, c)
                k = PMM[item]
                tensor.wait_ge(S["vFreePmm"], k - 2)
                half = mmslot(k)
                for dg in range(DG):
                    tensor.wait_ge(S["vSq"], cn * 8 + dg + 1)
                    ins = tensor.matmul(
                        pMM[0:1, half:half + TC], sOnesC[:, 0:1],
                        tsq[dg % 2][:, 0:TC],
                        start=(dg == 0), stop=(dg == DG - 1))
                    ins.then_inc(S["sPmm" if dg == DG - 1 else "pSq"], 1)
            elif kind == "mrep":
                _, p, c = item
                grp(item, 1, lambda i, half, st, sp: tensor.matmul(
                    pMM[:, half:half + TC], sOnesR[:, :],
                    sRowR[0:1, 0:TC], start=st, stop=sp),
                    pre=lambda cn=ch(p, c): tensor.wait_ge(S["aStat"], cn + 1))
            elif kind == "irep":
                grp(item, 1, lambda i, half, st, sp: tensor.matmul(
                    pMM[:, half:half + TC], sOnesR[:, :],
                    sRowR[0:1, TC:2 * TC], start=st, stop=sp))
            elif kind == "log":
                _, p, tt = item
                pre = None
                if tt == 0:
                    def pre(p=p):
                        if p == 0:
                            tensor.wait_ge(S["sB2l"], 32)
                        tensor.wait_ge(S["sX1L"], 16 * (p + 1))
                        tensor.wait_ge(S["sX1bL"], 16 * (p + 1))
                grp(item, DG, lambda i, half, st, sp, tt=tt: tensor.matmul(
                    pMM[:, half:half + 16],
                    sX1b[:, i * TP + tt * 128:i * TP + (tt + 1) * 128],
                    sGW[:, i * 16:(i + 1) * 16], start=st, stop=sp), pre=pre)
            elif kind == "gtr":
                _, p, j = item
                grp(item, 1, lambda i, half, st, sp, j=j: tensor.transpose(
                    pMM[0:16, half:half + 128],
                    sGfin[:, j * 16:(j + 1) * 16], sEyeF[:, :]),
                    pre=lambda p=p, j=j: tensor.wait_ge(S["vGfin"], 8 * p + j + 1))
            elif kind == "b2":
                _, p, dg, sb = item
                pre = None
                if dg == 0 and sb == 0:
                    def pre(p=p):
                        if p == 0:
                            tensor.wait_ge(S["sB2l"], 16)
                        tensor.wait_ge(S["vGateT"], p + 1)
                grp(item, 1, lambda i, half, st, sp, dg=dg, sb=sb: tensor.matmul(
                    pMM[:, half:half + TC], sB2[:, dg * 128:(dg + 1) * 128],
                    sGateT[:, sb * TC:(sb + 1) * TC], start=st, stop=sp), pre=pre)
            elif kind == "grep":
                _, p, e, hf = item
                grp(item, 1, lambda i, half, st, sp, e=e, hf=hf: tensor.matmul(
                    pMM[:, half:half + TC],
                    sOneH[:, e * 128:(e + 1) * 128],
                    sGateT[:, hf * TC:(hf + 1) * TC], start=st, stop=sp))
            elif kind == "ph1":
                _, p, e, hh = item
                ge = p * E + e
                for hc in range(hh * 8, hh * 8 + 8):
                    kcol = ge * HCN + hc
                    tensor.wait_ge(S["sW1l"], 16 * (kcol + 1))
                    wslot = kcol % R1
                    for sb in range(SUB):
                        gh = (ge * HCN + hc) * SUB + sb
                        if gh >= 2:
                            tensor.wait_ge(S["aFreeH"], gh - 1)
                        for dg in range(DG):
                            ins = tensor.matmul(
                                pH[:, (gh % 2) * 512:(gh % 2) * 512 + TC],
                                sW1[:, wslot * 1024 + dg * 128:wslot * 1024 + (dg + 1) * 128],
                                sX1b[:, dg * TP + sb * TC:dg * TP + (sb + 1) * TC],
                                start=(dg == 0), stop=(dg == DG - 1))
                            if dg == DG - 1:
                                ins.then_inc(S["pHgrp"], 1)
                            elif extra:
                                sem, val = extra.pop(0)
                                ins.then_inc(sem, val)
                    extra.append((S["fW1"], 1))
            elif kind == "ph2":
                _, p, e, hh = item
                ge = p * E + e
                if hh == 0:
                    tensor.wait_ge(S["aFreeH"], 32 * (ge + 1))
                for dg in range(hh * 4, hh * 4 + 4):
                    kcol = ge * DG + dg
                    tensor.wait_ge(S["sW2l"], 16 * (kcol + 1))
                    wslot = kcol % R2
                    for sb in range(SUB):
                        gy = (ge * DG + dg) * SUB + sb
                        if gy >= 2:
                            tensor.wait_ge(S["vFreeY"], gy - 1)
                        for hc in range(HCN):
                            ins = tensor.matmul(
                                pY[:, (gy % 2) * 512:(gy % 2) * 512 + TC],
                                sW2[:, wslot * 2048 + hc * 128:wslot * 2048 + (hc + 1) * 128],
                                sH[:, hc * TP + sb * TC:hc * TP + (sb + 1) * TC],
                                start=(hc == 0), stop=(hc == HCN - 1))
                            if hc == HCN - 1:
                                ins.then_inc(S["pYgrp"], 1)
                            elif extra:
                                sem, val = extra.pop(0)
                                ins.then_inc(sem, val)
                    extra.append((S["fW2"], 1))
            elif kind == "otr":
                _, p, tg, dg = item
                pre = None
                if tg == 0 and dg == 0:
                    pre = lambda p=p: tensor.wait_ge(S["vFreeY"], 16 * E * (p + 1))
                grp(item, 1, lambda i, half, st, sp, tg=tg, dg=dg:
                    tensor.transpose(
                        pMM[:, half:half + 128],
                        sYA[:, dg * TP + tg * 128:dg * TP + (tg + 1) * 128],
                        sEyeF[:, :]), pre=pre)

    @block.vector
    def _(vector):
        vvn = [0]

        def vping(ins):
            vvn[0] += 1
            ins.then_inc(S["vv"], 1)

        def vwait():
            vector.wait_ge(S["vv"], vvn[0])

        def half_of(key):
            return mmslot(PMM[key])

        vector.wait_ge(S["sConst"], 16 * NCONST)
        vector.memset(sDum[:, :], 0.0)
        vector.reciprocal(sInvA[:, :], sAnat[:, :])

        defq = []

        def flush(kmax=99):
            while defq and kmax > 0:
                defq.pop(0)()
                kmax -= 1

        for item in sch:
            kind = item[0]
            if kind == "ph1":
                flush(8)
                continue
            if kind == "b1i":
                j = item[1]
                vector.wait_ge(S["sPmm"], PMM[item] + 1)
                h = half_of(item)
                vector.tensor_copy(
                    sB1c[:, j * 16:(j + 1) * 16], pMM[:, h:h + 16],
                ).then_inc(S["vFreePmm"], 1)
            elif kind == "wbi":
                j = item[1]
                vector.wait_ge(S["sPmm"], PMM[item] + 1)
                h = half_of(item)
                dst = sWBc if j < 8 else sWCc
                dgj = j % 8
                vector.tensor_copy(
                    dst[:, dgj * 16:(dgj + 1) * 16], pMM_r[:, h:h + 16],
                ).then_inc(S["vFreePmm"], 1)
            elif kind == "xp":
                _, p, c, q, dg = item
                if q == 0 and dg == 0 and ch(p, c) > 0:
                    prev = (p, c - 1) if c > 0 else (p - 1, NCH - 1)
                    vector.wait_ge(S["sPmm"], PMM[("zt",) + prev] + 1)
                vector.wait_ge(S["sPmm"], PMM[item] + 1)
                h = half_of(item)
                vector.tensor_copy(
                    sXT[:, dg * TC + q * 128:dg * TC + (q + 1) * 128],
                    pMM_r[:, h:h + 128],
                ).then_inc(S["vFreePmm"], 1)
            elif kind in ("bmt", "cmt"):
                _, p, c = item
                vector.wait_ge(S["sPmm"], PMM[item] + 1)
                h = half_of(item)
                dst, bias = (sBmT, sBB) if kind == "bmt" else (sCmT, sBC)
                vector.tensor_scalar(
                    dst[:, :], pMM[0:16, h:h + TC], bias[:, 0:1], None, ALU.add,
                ).then_inc(S["vFreePmm"], 1)
            elif kind == "zt":
                _, p, c = item
                cn = ch(p, c)
                vector.wait_ge(S["sPmm"], PMM[item] + 1)
                h = half_of(item)
                vector.tensor_scalar(
                    sRowF[0:1, 0:TC], pMM[0:1, h:h + TC],
                    sBD[0:1, 0:1], sBD[0:1, 1:2], ALU.add, ALU.add,
                ).then_inc(S["vFreePmm"], 1)
                vector.tensor_copy(sDum[0:1, 0:1], sDum[0:1, 1:2]
                                   ).then_inc(S["vZrow"], 1)
                vector.wait_ge(S["aSp"], cn + 1)
                vector.tensor_scalar_max(
                    sRowF[0:1, 0:TC], sRowF[0:1, 0:TC], 0.0)
                vector.tensor_add(
                    sRowR[0:1, 0:TC], sRowF[0:1, 0:TC], sRowF[0:1, TC:2 * TC],
                ).then_inc(S["vDrow"], 1)
                if cn % 4 == 0:
                    vping(vector.memset(sCarry[:, :], 0.0))
                    vwait()
            elif kind == "scan":
                _, p, c, n = item
                cn = ch(p, c)
                abase = cn * 128
                k = PMM[item]
                hb = mmslot(k)
                hcr = mmslot(k + 1)
                flush()
                vector.wait_ge(S["sPmm"], k + 2)
                par = n % 2
                vector.tensor_copy(sBrep[:, par * TC:(par + 1) * TC],
                                   pMM[:, hb:hb + TC],
                                   ).then_inc(S["vFreePmm"], 1)
                vector.tensor_copy(sCrep[:, par * TC:(par + 1) * TC],
                                   pMM[:, hcr:hcr + TC],
                                   ).then_inc(S["vFreePmm"], 1)

                def dgstep(dg, p=p, c=c, n=n, cn=cn, abase=abase, par=par):
                    if n == 0 and dg == 0 and cn >= 1:
                        vector.wait_ge(S["x1s"], 16 * cn)
                    gl = n * 8 + dg
                    col = dg * 16 + n
                    vector.wait_ge(S["aAbar"], abase + gl + 1)
                    vector.scalar_tensor_tensor(
                        t1w[:, :], sAbar[:, (gl % 4) * TC:(gl % 4 + 1) * TC],
                        1.0, sXT[:, dg * TC:(dg + 1) * TC].bitcast(F32),
                        ALU.subtract, ALU.mult)
                    vector.scalar_tensor_tensor(
                        t2w[:, :], t1[:, :], sInvA[:, col:col + 1],
                        sBrep[:, par * TC:(par + 1) * TC],
                        ALU.mult, ALU.mult)
                    vector.tensor_tensor_scan(
                        sT3[:, :], sAbar[:, (gl % 4) * TC:(gl % 4 + 1) * TC],
                        t2[:, :], sCarry[:, gl:gl + 1], ALU.mult, ALU.add)
                    vector.tensor_tensor(
                        t2w[:, :], sT3[:, :],
                        sCrep[:, par * TC:(par + 1) * TC], ALU.mult)
                    dst = sYM[:, dg * TC:(dg + 1) * TC]
                    if n == 0:
                        ins = vector.tensor_add(
                            dst, t2[:, :], sXT[:, dg * TC:(dg + 1) * TC].bitcast(F32))
                    else:
                        ins = vector.tensor_add(dst, dst.bitcast(F32), t2[:, :])
                        if n == NST - 1:
                            ins.then_inc(S["vMix"], 1)
                    vector.tensor_copy(
                        sCarry[:, gl:gl + 1], sT3[:, TC - 1:TC],
                    ).then_inc(S["vAbar"], 1)
                for dg in range(DG):
                    defq.append(lambda dg=dg, f=dgstep: f(dg))
            elif kind == "sx":
                _, p, c = item
                cn = ch(p, c)
                flush()
                for dg in range(DG):
                    if dg >= 2:
                        vector.wait_ge(S["pSq"], cn * 7 + dg - 1)
                    src = sYM[:, dg * TC:(dg + 1) * TC].bitcast(F32)
                    vector.tensor_tensor(
                        tsq[dg % 2][:, 0:TC], src, src, ALU.mult,
                    ).then_inc(S["vSq"], 1)
            elif kind == "sxsq":
                _, p, c = item
                vector.wait_ge(S["sPmm"], PMM[item] + 1)
                hsx = half_of(("sx", p, c))
                hsq = half_of(item)
                vector.tensor_scalar_mul(
                    sRowR[0:1, 0:TC], pMM[0:1, hsx:hsx + TC], 1.0 / D,
                ).then_inc(S["vFreePmm"], 1)
                vector.tensor_tensor(
                    sRowF[0:1, TC:2 * TC], sRowR[0:1, 0:TC].bitcast(F32),
                    sRowR[0:1, 0:TC].bitcast(F32), ALU.mult)
                vector.tensor_scalar_mul(
                    sRowF[0:1, 2 * TC:3 * TC], pMM[0:1, hsq:hsq + TC], 1.0 / D,
                ).then_inc(S["vFreePmm"], 1)
                vector.tensor_sub(
                    sRowF[0:1, 2 * TC:3 * TC], sRowF[0:1, 2 * TC:3 * TC],
                    sRowF[0:1, TC:2 * TC])
                vector.tensor_scalar_add(
                    sRowF[0:1, 2 * TC:3 * TC], sRowF[0:1, 2 * TC:3 * TC], LNEPS,
                ).then_inc(S["vStat"], 1)
            elif kind == "irep":
                _, p, c = item
                vector.wait_ge(S["sPmm"], PMM[item] + 1)
                hm = half_of(("mrep", p, c))
                hi = half_of(item)
                for dg in range(DG):
                    dst = sYM[:, dg * TC:(dg + 1) * TC]
                    ins = vector.tensor_tensor(
                        t1w[:, :], dst.bitcast(F32), pMM[:, hm:hm + TC],
                        ALU.subtract)
                    if dg == DG - 1:
                        ins.then_inc(S["vFreePmm"], 1)
                    ins = vector.tensor_tensor(
                        dst, t1[:, :], pMM[:, hi:hi + TC], ALU.mult)
                    if dg == DG - 1:
                        ins.then_inc(S["vFreePmm"], 1)
                vector.tensor_copy(sDum[0:1, 0:1], sDum[0:1, 1:2]
                                   ).then_inc(S["vLn1"], 1)
            elif kind == "log":
                _, p, tt = item
                vector.wait_ge(S["sPmm"], PMM[item] + 1)
                hl = half_of(item)
                vector.tensor_copy(
                    sGw1[:, tt * 16:(tt + 1) * 16], pMM[:, hl:hl + 16],
                ).then_inc(S["vFreePmm"], 1)
                if tt == 7:
                    g1 = [sGw1[:, t * 16:(t + 1) * 16] for t in range(8)]
                    g2 = [sGw2[:, t * 16:(t + 1) * 16] for t in range(8)]
                    g3 = [sGw3[:, t * 16:(t + 1) * 16] for t in range(8)]

                    def col(s):
                        return [sGcol[:, s * 8 + t:s * 8 + t + 1]
                                for t in range(8)]
                    for t in range(8):
                        vector.reduce_max(col(0)[t], g1[t], axis=AX.X)
                    for t in range(8):
                        ins = vector.tensor_scalar(
                            g3[t], g1[t], col(0)[t], None, ALU.subtract)
                    ins.then_inc(S["vGexp"], 1)
                    vector.wait_ge(S["aGexp"], p + 1)
                    for t in range(8):
                        vector.reduce_sum(col(1)[t], g2[t], axis=AX.X)
                    for t in range(8):
                        vector.reciprocal(col(2)[t], col(1)[t])
                    for t in range(8):
                        vector.tensor_scalar(
                            g1[t], g2[t], col(2)[t], None, ALU.mult)
                    for t in range(8):
                        vector.reduce_max(col(3)[t], g1[t], axis=AX.X)
                    for t in range(8):
                        vector.tensor_scalar(
                            g2[t], g1[t], col(3)[t], None, ALU.is_equal)
                    for t in range(8):
                        vector.scalar_tensor_tensor(
                            g3[t], g2[t], -2.0, g1[t], ALU.mult, ALU.add)
                    for t in range(8):
                        vector.reduce_max(col(4)[t], g3[t], axis=AX.X)
                    for t in range(8):
                        vector.tensor_scalar(
                            g2[t], g3[t], col(4)[t], None, ALU.is_equal)
                    for t in range(8):
                        vector.scalar_tensor_tensor(
                            g3[t], g2[t], -2.0, g3[t], ALU.mult, ALU.add)
                    for t in range(8):
                        vector.reduce_max(col(5)[t], g3[t], axis=AX.X)
                    for t in range(8):
                        vector.tensor_scalar(
                            g2[t], g1[t], col(5)[t], None, ALU.is_ge)
                    for t in range(8):
                        vector.tensor_mul(g1[t], g1[t], g2[t])
                    for t in range(8):
                        vector.reduce_sum(col(6)[t], g1[t], axis=AX.X)
                    for t in range(8):
                        vector.tensor_scalar_max(col(6)[t], col(6)[t], GEPS)
                    for t in range(8):
                        vector.reciprocal(col(7)[t], col(6)[t])
                    for t in range(8):
                        vector.tensor_scalar(
                            sGfin[:, t * 16:(t + 1) * 16], g1[t],
                            col(7)[t], None, ALU.mult,
                        ).then_inc(S["vGfin"], 1)
            elif kind == "gtr":
                _, p, j = item
                vector.wait_ge(S["sPmm"], PMM[item] + 1)
                h = half_of(item)
                vector.tensor_copy(
                    sGateT[:, j * 128:(j + 1) * 128], pMM[0:16, h:h + 128],
                ).then_inc(S["vFreePmm"], 1)
                if j == 7:
                    vector.tensor_copy(sDum[0:1, 0:1], sDum[0:1, 1:2]
                                       ).then_inc(S["vGateT"], 1)
            elif kind == "b2":
                _, p, dg, sb = item
                if dg == 0 and sb == 0:
                    vector.wait_ge(S["sX1L"], 16 * (p + 1))
                vector.wait_ge(S["sPmm"], PMM[item] + 1)
                hb2 = half_of(item)
                vector.tensor_add(
                    sYA[:, dg * TP + sb * TC:dg * TP + (sb + 1) * TC],
                    pMM[:, hb2:hb2 + TC],
                    shx1[:, dg * TP + sb * TC:dg * TP + (sb + 1) * TC],
                ).then_inc(S["vFreePmm"], 1)
            elif kind == "grep":
                _, p, e, hf = item
                vector.wait_ge(S["sPmm"], PMM[item] + 1)
                hg = half_of(item)
                vector.tensor_copy(
                    sGRep[:, hf * TC:(hf + 1) * TC], pMM[:, hg:hg + TC],
                ).then_inc(S["vFreePmm"], 1)
            elif kind == "ph2":
                _, p, e, hh = item
                ge = p * E + e
                for dg in range(hh * 4, hh * 4 + 4):
                    for sb in range(SUB):
                        gy = (ge * DG + dg) * SUB + sb
                        if sb == 0 and defq:
                            defq.pop(0)()
                        vector.wait_ge(S["pYgrp"], gy + 1)
                        vector.tensor_tensor(
                            t1w[:, :], pY[:, (gy % 2) * 512:(gy % 2) * 512 + TC],
                            sGRep[:, sb * TC:(sb + 1) * TC], ALU.mult)
                        dst = sYA[:, dg * TP + sb * TC:dg * TP + (sb + 1) * TC]
                        vector.tensor_add(dst, dst, t1[:, :],
                                          ).then_inc(S["vFreeY"], 1)
            elif kind == "otr":
                _, p, tg, dg = item
                g = p * 8 + tg
                ob = (g % 2) * D
                if dg == 0 and g >= 2:
                    flush()
                    vector.wait_ge(S["st"], 16 * (g - 1))
                vector.wait_ge(S["sPmm"], PMM[item] + 1)
                h = half_of(item)
                vector.tensor_copy(
                    sOutT[:, ob + dg * 128:ob + (dg + 1) * 128],
                    pMM[:, h:h + 128],
                ).then_inc(S["vFreePmm"], 1)
            elif kind == "ln2":
                _, p, tg = item
                g = p * 8 + tg
                ob = (g % 2) * D

                def ln2chain(g=g, ob=ob):
                    oT = sOutT[:, ob:ob + D]
                    vector.tensor_tensor(
                        t1w[:, :], oT[:, 0:TC], oT[:, 0:TC], ALU.mult)
                    vector.tensor_tensor(
                        t2w[:, :], oT[:, TC:2 * TC], oT[:, TC:2 * TC], ALU.mult)
                    vector.reduce_sum(sL2c[:, 0:1], oT[:, 0:TC], axis=AX.X)
                    vping(vector.reduce_sum(sL2c[:, 1:2], oT[:, TC:2 * TC], axis=AX.X))
                    vwait()
                    vector.tensor_add(sL2c[:, 0:1], sL2c[:, 0:1], sL2c[:, 1:2])
                    vector.reduce_sum(sL2c[:, 2:3], t1[:, :], axis=AX.X)
                    vping(vector.reduce_sum(sL2c[:, 3:4], t2[:, :], axis=AX.X))
                    vwait()
                    vping(vector.tensor_add(sL2c[:, 2:3], sL2c[:, 2:3], sL2c[:, 3:4]))
                    vwait()
                    vping(vector.tensor_scalar_mul(sL2c[:, 4:5], sL2c[:, 0:1], 1.0 / D))
                    vwait()
                    vping(vector.tensor_mul(sL2c[:, 5:6], sL2c[:, 4:5], sL2c[:, 4:5]))
                    vwait()
                    vping(vector.tensor_scalar_mul(sL2c[:, 6:7], sL2c[:, 2:3], 1.0 / D))
                    vwait()
                    vping(vector.tensor_sub(sL2c[:, 6:7], sL2c[:, 6:7], sL2c[:, 5:6]))
                    vwait()
                    vector.tensor_scalar_add(
                        sL2c[:, 6:7], sL2c[:, 6:7], LNEPS,
                    ).then_inc(S["vLn2"], 1)
                    vector.wait_ge(S["aLn2"], g + 1)
                    vector.tensor_scalar(
                        oT, oT, sL2c[:, 4:5], sL2c[:, 7:8],
                        ALU.subtract, ALU.mult,
                    ).then_inc(S["vOut"], 1)
                ln2chain()

    @block.scalar
    def _(scalar):
        adefq = []

        def aflush(kmax=99):
            while adefq and kmax > 0:
                adefq.pop(0)()
                kmax -= 1

        scalar.wait_ge(S["sConst"], 16 * NCONST)
        for item in sch:
            kind = item[0]
            if kind == "xin":
                _, p, c, q = item
                aflush()
                Q = ch(p, c) * 4 + q
                if Q > 0:
                    pq = Q - 1
                    key = ("xp", pq // 8, (pq // 4) % 2, pq % 4, DG - 1)
                    scalar.wait_ge(S["sPmm"], PMM[key] + 1)
                row = p * TP + c * TC + q * 128
                scalar.dma_start(
                    out=sInX[:, :], in_=x_d[row:row + 128, :]
                ).then_inc(S["sXq"], 16)
            elif kind == "zt":
                _, p, c = item
                cn = ch(p, c)
                aflush()
                scalar.wait_ge(S["vZrow"], cn + 1)
                scalar.activation(
                    sRowF[0:1, TC:2 * TC], sRowF[0:1, 0:TC], AF.Abs)
                scalar.activation(
                    sRowF[0:1, 2 * TC:3 * TC], sRowF[0:1, TC:2 * TC],
                    AF.Exp, scale=-1.0)
                scalar.activation(
                    sRowF[0:1, TC:2 * TC], sRowF[0:1, 2 * TC:3 * TC],
                    AF.Ln, bias=1.0,
                ).then_inc(S["aSp"], 1)
            elif kind == "scan":
                _, p, c, n = item
                cn = ch(p, c)
                abase = cn * 128
                aflush()
                if n == 0:
                    scalar.wait_ge(S["pDRs"], cn + 1)
                for dg in range(DG):
                    gl = n * 8 + dg
                    col = dg * 16 + n
                    if gl >= 4:
                        scalar.wait_ge(S["vAbar"], abase + gl - 3)
                    scalar.activation(
                        sAbar[:, (gl % 4) * TC:(gl % 4 + 1) * TC],
                        pDR[:, 0:TC], AF.Exp, scale=sAnat[:, col:col + 1],
                    ).then_inc(S["aAbar"], 1)
                if n == NST - 1:
                    scalar.copy(sDum[0:1, 2:3], sDum[0:1, 3:4]
                                ).then_inc(S["aFreeDR"], 1)
            elif kind == "sxsq":
                _, p, c = item
                cn = ch(p, c)
                aflush()
                scalar.wait_ge(S["vStat"], cn + 1)
                scalar.activation(
                    sRowF[0:1, TC:2 * TC], sRowF[0:1, 2 * TC:3 * TC], AF.Ln)
                scalar.activation(
                    sRowR[0:1, TC:2 * TC],
                    sRowF[0:1, TC:2 * TC], AF.Exp, scale=-0.5,
                ).then_inc(S["aStat"], 1)
            elif kind == "x1st":
                _, p, c = item
                cn = ch(p, c)
                scalar.wait_ge(S["vLn1"], cn + 1)
                scalar.dma_start(
                    out=x1scr_d[p * 128:(p + 1) * 128, :].rearrange(
                        "p (dg t) -> p dg t", dg=DG)[:, :, c * TC:(c + 1) * TC],
                    in_=sYM[:, :].rearrange("p (dg t) -> p dg t", dg=DG),
                ).then_inc(S["x1s"], 16)
            elif kind == "log":
                _, p, tt = item
                aflush()
                if tt == 7:
                    scalar.wait_ge(S["vGexp"], p + 1)
                    scalar.activation(
                        sGw2[:, :], sGw3[:, :], AF.Exp,
                    ).then_inc(S["aGexp"], 1)
            elif kind == "ph1":
                _, p, e, hh = item
                ge = p * E + e
                for hc in range(hh * 8, hh * 8 + 8):
                    if hc % 4 == 2:
                        aflush(1)
                    for sb in range(SUB):
                        gh = (ge * HCN + hc) * SUB + sb
                        if e == 0 and hc == 0 and sb == 0:
                            scalar.wait_ge(
                                S["vFreePmm"],
                                PMM[("b2", p, DG - 1, SUB - 1)] + 1)
                        if hc == 0 and sb == 0:
                            scalar.wait_ge(S["pYgrp"], 16 * ge)
                        scalar.wait_ge(S["pHgrp"], gh + 1)
                        scalar.activation(
                            sH[:, hc * TP + sb * TC:hc * TP + (sb + 1) * TC],
                            pH[:, (gh % 2) * 512:(gh % 2) * 512 + TC],
                            AF.Relu, bias=sB1c[:, hc * 16 + e:hc * 16 + e + 1],
                        ).then_inc(S["aFreeH"], 1)
            elif kind == "ln2":
                _, p, tg = item
                g = p * 8 + tg

                def achain(g=g):
                    scalar.wait_ge(S["vLn2"], g + 1)
                    scalar.activation(sL2c[:, 6:7], sL2c[:, 6:7], AF.Ln)
                    scalar.activation(
                        sL2c[:, 7:8], sL2c[:, 6:7], AF.Exp, scale=-0.5,
                    ).then_inc(S["aLn2"], 1)
                achain()

    ctx.close()
    return nc


def _consts():
    eye = np.eye(128, dtype=np.float32)
    onescol = np.ones((128, 1), dtype=np.float32)
    onesrow = np.ones((1, 128), dtype=np.float32)
    onehot = np.zeros((E, E * 128), dtype=np.float32)
    for e in range(E):
        onehot[e, e * 128:(e + 1) * 128] = 1.0
    import ml_dtypes
    return {
        "eyer": eye, "eyef": eye, "onescol": onescol, "onesrow": onesrow,
        "onehot": onehot.astype(ml_dtypes.bfloat16),
    }


def make_in_map(core, inputs):
    x = inputs["L0_attn_sequence"]
    f = np.float32
    m = {
        "x": np.ascontiguousarray(
            x[core * BPC:(core + 1) * BPC].reshape(TOK, D).astype(f)),
        "amat": np.ascontiguousarray(inputs["L0_attn_A"].astype(f)),
        "wb": np.ascontiguousarray(inputs["L0_attn_W_B"].astype(f)),
        "bb": np.ascontiguousarray(
            inputs["L0_attn_b_B"].astype(f).reshape(NST, 1)),
        "wc": np.ascontiguousarray(inputs["L0_attn_W_C"].astype(f)),
        "bc": np.ascontiguousarray(
            inputs["L0_attn_b_C"].astype(f).reshape(NST, 1)),
        "wd": np.ascontiguousarray(
            inputs["L0_attn_W_Delta"].astype(f).reshape(D, 1)),
        "bd": np.ascontiguousarray(
            inputs["L0_attn_b_Delta"].astype(f).reshape(1, 1)),
        "biasd": np.ascontiguousarray(
            inputs["L0_attn_bias_Delta"].astype(f).reshape(1, 1)),
        "gw": np.ascontiguousarray(inputs["L0_ffn_gate"].astype(f)),
        "w1": np.ascontiguousarray(inputs["L0_ffn_W1"].astype(f)),
        "b1": np.ascontiguousarray(inputs["L0_ffn_b1"].astype(f)),
        "w2": np.ascontiguousarray(inputs["L0_ffn_W2"].astype(f)),
        "b2": np.ascontiguousarray(inputs["L0_ffn_b2"].astype(f)),
    }
    m.update(_consts())
    return m


def kernel(**inputs):
    nc = build_program(4)
    in_maps = [make_in_map(core, inputs) for core in range(NCORES)]
    res = run_bass_kernel_spmd(nc, in_maps, core_ids=list(range(NCORES)))
    out = np.concatenate(
        [r["y"].reshape(BPC, T, D) for r in res.results], axis=0)
    return out.astype(np.float32)


# version 08
# speedup vs baseline: 1.0589x; 1.0589x over previous
import contextlib

import numpy as np

import concourse.bass as bass
import concourse.mybir as mybir
from concourse.bass_utils import run_bass_kernel_spmd

F32 = mybir.dt.float32
F32R = mybir.dt.float32r
BF16 = mybir.dt.bfloat16
ALU = mybir.AluOpType
AF = mybir.ActivationFunctionType
AX = mybir.AxisListType

D = 1024
NST = 16
E = 16
HID = 2048
NCORES = 8
BPC = 2
T = 2048
TOK = BPC * T
TP = 1024
NCH = 2
TC = 512
DG = 8
HCN = 16
SUB = 2
R1 = 4
R2 = 4
LNEPS = 1e-5
GEPS = 1e-12
NCONST = 13


def build_schedule(NP):
    sch = []
    for j in range(16):
        sch.append(("b1i", j))
    for j in range(16):
        sch.append(("wbi", j))

    def mixer_units(p, c):
        units = []
        for q in range(4):
            units.append([("xin", p, c, q)] +
                         [("xp", p, c, q, dg) for dg in range(DG)])
        units.append([("bmt", p, c)])
        units.append([("cmt", p, c)])
        units.append([("zt", p, c)])
        units.append([("drep", p, c)])
        for n in range(NST):
            units.append([("scan", p, c, n)])
        units.append([("sx", p, c), ("sxsq", p, c),
                      ("mrep", p, c), ("irep", p, c)])
        units.append([("x1st", p, c)])
        return units

    for u in mixer_units(0, 0) + mixer_units(0, 1):
        sch.extend(u)

    uq = []
    for p in range(NP):
        sch.append(("reload", p))
        if p >= 1:
            for tg in range(8):
                for dg in range(DG):
                    sch.append(("otr", p - 1, tg, dg))
                sch.append(("ln2", p - 1, tg))
        for tt in range(8):
            sch.append(("log", p, tt))
        for j in range(8):
            sch.append(("gtr", p, j))
        for dg in range(DG):
            for sb in range(SUB):
                sch.append(("b2", p, dg, sb))
        if p + 1 < NP:
            for u in mixer_units(p + 1, 0) + mixer_units(p + 1, 1):
                uq.append((p + 1, u))

        UCOST = {"xin": 5, "bmt": 1, "cmt": 1, "zt": 3, "drep": 1,
                 "scan": 30, "sx": 12, "x1st": 1}
        total = sum(UCOST[u[0][0]] for _, u in uq)
        tk = [0]
        SKIP = 12
        WTOT = E * 10 - SKIP

        def take_to(w):
            target = total * max(0, w - SKIP) // WTOT
            while uq and tk[0] < target:
                _, u = uq.pop(0)
                sch.extend(u)
                tk[0] += UCOST[u[0][0]]
        for e in range(E):
            sch.append(("grep", p, e, 0))
            sch.append(("grep", p, e, 1))
            sch.append(("ph1", p, e, 0))
            take_to(e * 10 + 3)
            sch.append(("ph1", p, e, 1))
            take_to(e * 10 + 6)
            sch.append(("ph2", p, e, 0))
            take_to(e * 10 + 8)
            sch.append(("ph2", p, e, 1))
            take_to(e * 10 + 10)
        while uq and uq[0][0] <= p + 1:
            _, u = uq.pop(0)
            sch.extend(u)
    for tg in range(8):
        for dg in range(DG):
            sch.append(("otr", NP - 1, tg, dg))
        sch.append(("ln2", NP - 1, tg))
    return sch


PMM_KINDS = {"b1i", "wbi", "xp", "bmt", "cmt", "zt", "scan",
             "sx", "sxsq", "mrep", "irep", "log", "gtr", "b2", "grep", "otr"}


def build_program(NP=4, race_detect=True, taps=False):
    sch = build_schedule(NP)
    PMM = {}
    ctr = 0
    for item in sch:
        if item[0] in PMM_KINDS:
            PMM[item] = ctr
            ctr += 2 if item[0] == "scan" else 1
    nc = bass.Bass("TRN2", target_bir_lowering=False,
                   detect_race_conditions=race_detect)

    x_d = nc.dram_tensor("x", [TOK, D], F32R, kind="ExternalInput").ap()
    a_d = nc.dram_tensor("amat", [D, NST], F32, kind="ExternalInput").ap()
    wb_d = nc.dram_tensor("wb", [NST, D], F32R, kind="ExternalInput").ap()
    bb_d = nc.dram_tensor("bb", [NST, 1], F32, kind="ExternalInput").ap()
    wc_d = nc.dram_tensor("wc", [NST, D], F32R, kind="ExternalInput").ap()
    bc_d = nc.dram_tensor("bc", [NST, 1], F32, kind="ExternalInput").ap()
    wd_d = nc.dram_tensor("wd", [D, 1], F32R, kind="ExternalInput").ap()
    bd_d = nc.dram_tensor("bd", [1, 1], F32, kind="ExternalInput").ap()
    biasd_d = nc.dram_tensor("biasd", [1, 1], F32, kind="ExternalInput").ap()
    gw_d = nc.dram_tensor("gw", [D, E], F32R, kind="ExternalInput").ap()
    w1_d = nc.dram_tensor("w1", [E, D, HID], F32R, kind="ExternalInput").ap()
    b1_d = nc.dram_tensor("b1", [E, HID], F32, kind="ExternalInput").ap()
    w2_d = nc.dram_tensor("w2", [E, HID, D], F32, kind="ExternalInput").ap()
    b2_d = nc.dram_tensor("b2", [E, D], F32, kind="ExternalInput").ap()
    eyer_d = nc.dram_tensor("eyer", [128, 128], F32R, kind="ExternalInput").ap()
    eyef_d = nc.dram_tensor("eyef", [128, 128], F32, kind="ExternalInput").ap()
    onescol_d = nc.dram_tensor("onescol", [128, 1], F32R, kind="ExternalInput").ap()
    onesrow_d = nc.dram_tensor("onesrow", [1, 128], F32R, kind="ExternalInput").ap()
    onehot_d = nc.dram_tensor("onehot", [E, E * 128], BF16, kind="ExternalInput").ap()
    y_d = nc.dram_tensor("y", [TOK, D], F32, kind="ExternalOutput").ap()
    x1scr_d = nc.dram_tensor("x1scr", [NP * 128, DG * TP], F32R, kind="Internal").ap()
    if taps:
        tap_d = {
            "t_x1": nc.dram_tensor("t_x1", [128, DG * TP], F32, kind="ExternalOutput").ap(),
            "t_ya": nc.dram_tensor("t_ya", [128, DG * TP], F32, kind="ExternalOutput").ap(),
            "t_gt": nc.dram_tensor("t_gt", [NST, TP], F32, kind="ExternalOutput").ap(),
        }

    ctx = contextlib.ExitStack()
    EE = ctx.enter_context

    sW1 = EE(nc.sbuf_tensor("sW1", [128, R1 * DG * 128], BF16))
    sW2 = EE(nc.sbuf_tensor("sW2", [128, R2 * HCN * 128], BF16))
    sH = EE(nc.sbuf_tensor("sH", [128, HCN * TP], BF16))
    sX1b = EE(nc.sbuf_tensor("sX1b", [128, DG * TP], BF16))
    sYM = EE(nc.sbuf_tensor("sYM", [128, DG * TC], F32R))
    sYA = EE(nc.sbuf_tensor("sYA", [128, DG * TP], F32))
    sXT = EE(nc.sbuf_tensor("sXT", [128, DG * TC], F32R))
    sInX = EE(nc.sbuf_tensor("sInX", [128, D], F32R))
    sOutT = EE(nc.sbuf_tensor("sOutT", [128, 2 * D], BF16))
    sAnat = EE(nc.sbuf_tensor("sAnat", [128, 128], F32))
    sInvA = EE(nc.sbuf_tensor("sInvA", [128, 128], F32))
    sCarry = EE(nc.sbuf_tensor("sCarry", [128, 128], F32))
    sAbar = EE(nc.sbuf_tensor("sAbar", [128, 4 * TC], BF16))
    sT1 = EE(nc.sbuf_tensor("sT1", [128, TC], F32R))
    sT2 = EE(nc.sbuf_tensor("sT2", [128, TC], F32R))
    sT3 = EE(nc.sbuf_tensor("sT3", [128, TC], F32))
    sWBc = EE(nc.sbuf_tensor("sWBc", [128, 128], F32R))
    sWCc = EE(nc.sbuf_tensor("sWCc", [128, 128], F32R))
    sWDc = EE(nc.sbuf_tensor("sWDc", [128, 8], F32R))
    sGW = EE(nc.sbuf_tensor("sGW", [128, 128], BF16))
    sB1c = EE(nc.sbuf_tensor("sB1c", [128, HCN * E], F32))
    sB2 = EE(nc.sbuf_tensor("sB2", [E, D], BF16))
    sGateT = EE(nc.sbuf_tensor("sGateT", [E, TP], BF16))
    sGRep = EE(nc.sbuf_tensor("sGRep", [128, TP], BF16))
    sGfin = EE(nc.sbuf_tensor("sGfin", [128, 128], F32))
    sGw1 = EE(nc.sbuf_tensor("sGw1", [128, 128], F32))
    sGw2 = EE(nc.sbuf_tensor("sGw2", [128, 128], F32))
    sGw3 = EE(nc.sbuf_tensor("sGw3", [128, 128], F32))
    sGcol = EE(nc.sbuf_tensor("sGcol", [128, 64], F32))
    sBmT = EE(nc.sbuf_tensor("sBmT", [NST, TC], BF16))
    sBrep = EE(nc.sbuf_tensor("sBrep", [128, 2 * TC], BF16))
    sCrep = EE(nc.sbuf_tensor("sCrep", [128, 2 * TC], BF16))
    sCmT = EE(nc.sbuf_tensor("sCmT", [NST, TC], BF16))
    sRowF = EE(nc.sbuf_tensor("sRowF", [1, 3 * TC], F32))
    sRowR = EE(nc.sbuf_tensor("sRowR", [1, 2 * TC], F32R))
    sBB = EE(nc.sbuf_tensor("sBB", [NST, 1], F32))
    sBC = EE(nc.sbuf_tensor("sBC", [NST, 1], F32))
    sBD = EE(nc.sbuf_tensor("sBD", [1, 2], F32))
    sL2c = EE(nc.sbuf_tensor("sL2c", [128, 8], F32))
    sEyeR = EE(nc.sbuf_tensor("sEyeR", [128, 128], F32R))
    sOnesC = EE(nc.sbuf_tensor("sOnesC", [128, 1], F32R))
    sOnesR = EE(nc.sbuf_tensor("sOnesR", [1, 128], F32R))
    sOneH = EE(nc.sbuf_tensor("sOneH", [E, E * 128], BF16))
    sDum = EE(nc.sbuf_tensor("sDum", [1, 8], F32))

    b1nat = sYA.ap()[0:16, 0:HID]
    wbnat = sXT.ap()[0:16, 0:1024]
    wcnat = sXT.ap()[0:16, 1024:2048]
    t1 = sT1.ap().bitcast(F32)
    t2 = sT2.ap().bitcast(F32)
    t1w = sT1.ap()
    t2w = sT2.ap()
    tsq = [sT1.ap(), sT2.ap()]

    pH = EE(nc.psum_tensor("pH", [128, 1024], F32))
    pY = EE(nc.psum_tensor("pY", [128, 1024], F32))
    pMM = EE(nc.psum_tensor("pMM", [128, 1536], F32))
    pDR = EE(nc.psum_tensor("pDR", [128, 512], F32))
    pMM_r = pMM.ap().bitcast(F32R)
    sEyeF = sEyeR.ap().bitcast(F32)
    shx1 = sH.ap().bitcast(F32)

    semnames = [
        "sConst", "sXq", "sW1l", "fW1", "sW2l", "fW2",
        "sPmm", "vFreePmm", "vZrow", "aSp", "vDrow",
        "vMix", "vSq", "pSq", "vStat", "aStat", "aAbar", "vAbar",
        "pHgrp", "aFreeH", "pYgrp", "vFreeY", "pX1done",
        "vGexp", "aGexp", "vGfin", "vGateT", "vLn1", "vLn2", "aLn2",
        "vOut", "st", "vv", "pDRs", "aFreeDR", "x1s", "sX1L", "sB2l", "sX1bL",
    ]
    S = {n: EE(nc.semaphore(n)) for n in semnames}
    block = EE(nc.Block())

    def ch(p, c):
        return p * NCH + c

    def mmslot(k):
        return (k % 3) * 512

    @block.sync
    def _(sync):
        sync.dma_start(out=sEyeR[:, :], in_=eyer_d).then_inc(S["sConst"], 16)
        sync.dma_start(out=sOnesC[:, :], in_=onescol_d).then_inc(S["sConst"], 16)
        sync.dma_start(out=sOnesR[:, :], in_=onesrow_d).then_inc(S["sConst"], 16)
        sync.dma_start(out=sOneH[:, :], in_=onehot_d).then_inc(S["sConst"], 16)
        sync.dma_start(
            out=sAnat[:, :].rearrange("p (dg n) -> p dg n", dg=DG),
            in_=a_d.rearrange("(dg p) n -> p dg n", p=128),
        ).then_inc(S["sConst"], 16)
        sync.dma_start(out=wbnat, in_=wb_d).then_inc(S["sConst"], 16)
        sync.dma_start(out=wcnat, in_=wc_d).then_inc(S["sConst"], 16)
        sync.dma_start(
            out=sWDc[:, :].rearrange("p (dg o) -> p dg o", dg=DG),
            in_=wd_d.rearrange("(dg p) o -> p dg o", p=128),
            allow_slow_non_contiguous=True,
        ).then_inc(S["sConst"], 16)
        sync.dma_start(out=sBB[:, :], in_=bb_d).then_inc(S["sConst"], 16)
        sync.dma_start(out=sBC[:, :], in_=bc_d).then_inc(S["sConst"], 16)
        sync.dma_start(out=sBD[:, 0:1], in_=bd_d).then_inc(S["sConst"], 16)
        sync.dma_start(out=sBD[:, 1:2], in_=biasd_d).then_inc(S["sConst"], 16)
        sync.dma_start(out=b1nat, in_=b1_d).then_inc(S["sConst"], 16)

        x1f = x1scr_d.bitcast(F32)
        for p in range(NP):
            if p >= 1:
                sync.wait_ge(S["pYgrp"], E * DG * SUB * p)
            sync.wait_ge(S["x1s"], 16 * (2 * p + 2))
            sync.dma_start(
                out=shx1[:, :], in_=x1f[p * 128:(p + 1) * 128, :]
            ).then_inc(S["sX1L"], 16)
        sync.wait_ge(S["st"], 16 * 8 * NP)

    @block.gpsimd
    def _(gpsimd):
        gpsimd.dma_start(out=sB2[:, :], in_=b2_d).then_inc(S["sB2l"], 16)
        gpsimd.dma_start(
            out=sGW[:, :].rearrange("p (dg e) -> p dg e", dg=DG),
            in_=gw_d.bitcast(F32).rearrange("(dg p) e -> p dg e", p=128),
        ).then_inc(S["sB2l"], 16)
        w1f = w1_d.bitcast(F32)

        def w1col(k):
            if k >= R1:
                gpsimd.wait_ge(S["fW1"], k - R1 + 1)
            slot = k % R1
            e, hc = divmod(k % (E * HCN), HCN)
            gpsimd.dma_start(
                out=sW1[:, slot * 1024:(slot + 1) * 1024].rearrange(
                    "p (dg h) -> p dg h", dg=DG),
                in_=w1f[e, :, hc * 128:(hc + 1) * 128].rearrange(
                    "(dg p) h -> p dg h", p=128),
            ).then_inc(S["sW1l"], 16)

        def w2col(k):
            if k >= R2:
                gpsimd.wait_ge(S["fW2"], k - R2 + 1)
            slot = k % R2
            e, dg = divmod(k % (E * DG), DG)
            p = k // (E * DG)
            gpsimd.dma_start(
                out=sW2[:, slot * 2048:(slot + 1) * 2048].rearrange(
                    "p (hc d) -> p hc d", hc=HCN),
                in_=w2_d[e, :, dg * 128:(dg + 1) * 128].rearrange(
                    "(hc p) d -> p hc d", p=128),
            ).then_inc(S["sW2l"], 16)

        for p in range(NP):
            base = p * E * DG
            w1base = p * E * HCN
            for k in range(w1base, w1base + R1):
                w1col(k)
            if p >= 1:
                for tg in range(8):
                    g = (p - 1) * 8 + tg
                    gpsimd.wait_ge(S["vOut"], g + 1)
                    row = (p - 1) * TP + tg * 128
                    gpsimd.dma_start(
                        out=y_d[row:row + 128, :],
                        in_=sOutT[:, (g % 2) * D:(g % 2 + 1) * D]
                    ).then_inc(S["st"], 16)
            if p >= 1:
                gpsimd.wait_ge(S["pYgrp"], E * DG * SUB * p)
            gpsimd.wait_ge(S["x1s"], 16 * (2 * p + 2))
            gpsimd.dma_start(
                out=sX1b[:, :],
                in_=x1scr_d.bitcast(F32)[p * 128:(p + 1) * 128, :]
            ).then_inc(S["sX1bL"], 16)
            for e in range(E):
                lo = w1base + e * HCN
                for k in range(max(lo, w1base + R1), lo + HCN):
                    w1col(k)
                for k in range(base + e * DG, base + (e + 1) * DG):
                    w2col(k)
        for tg in range(8):
            g = (NP - 1) * 8 + tg
            gpsimd.wait_ge(S["vOut"], g + 1)
            row = (NP - 1) * TP + tg * 128
            gpsimd.dma_start(
                out=y_d[row:row + 128, :],
                in_=sOutT[:, (g % 2) * D:(g % 2 + 1) * D]
            ).then_inc(S["st"], 16)
        if taps:
            gpsimd.wait_ge(S["st"], 16 * 8 * NP)
            gpsimd.dma_start(out=tap_d["t_x1"], in_=sX1b[:, :]).then_inc(S["st"], 16)
            gpsimd.dma_start(out=tap_d["t_ya"], in_=sYA[:, :]).then_inc(S["st"], 16)
            gpsimd.dma_start(out=tap_d["t_gt"], in_=sGateT.ap().bitcast(F32)).then_inc(S["st"], 16)
            gpsimd.wait_ge(S["st"], 16 * 8 * NP + 48)

    @block.tensor
    def _(tensor):
        extra = []

        def grp(key, n_mm, mk_mm, pre=None):
            k = PMM[key]
            if k >= 3:
                tensor.wait_ge(S["vFreePmm"], k - 2)
            if pre:
                pre()
            half = mmslot(k)
            for i in range(n_mm):
                ins = mk_mm(i, half, i == 0, i == n_mm - 1)
                if i == n_mm - 1:
                    ins.then_inc(S["sPmm"], 1)
                elif extra:
                    sem, val = extra.pop(0)
                    ins.then_inc(sem, val)

        tensor.wait_ge(S["sConst"], 16 * NCONST)
        for item in sch:
            kind = item[0]
            if kind == "b1i":
                j = item[1]
                grp(item, 1, lambda i, half, st, sp, j=j: tensor.transpose(
                    pMM[:, half:half + 16],
                    b1nat[:, j * 128:(j + 1) * 128], sEyeF[0:16, 0:16]))
            elif kind == "wbi":
                j = item[1]
                src = wbnat if j < 8 else wcnat
                dgj = j % 8
                grp(item, 1, lambda i, half, st, sp, src=src, dgj=dgj:
                    tensor.transpose(
                        pMM_r[:, half:half + 16],
                        src[:, dgj * 128:(dgj + 1) * 128], sEyeR[0:16, 0:16]))
            elif kind == "xp":
                _, p, c, q, dg = item
                if dg == 0:
                    Q = ch(p, c) * 4 + q
                    tensor.wait_ge(S["sXq"], 16 * (Q + 1))
                grp(item, 1, lambda i, half, st, sp, dg=dg: tensor.transpose(
                    pMM_r[:, half:half + 128],
                    sInX[:, dg * 128:(dg + 1) * 128], sEyeR[:, :]))
            elif kind in ("bmt", "cmt"):
                _, p, c = item
                w = sWBc if kind == "bmt" else sWCc
                pre = None
                if kind == "bmt":
                    pre = lambda p=p, c=c: tensor.wait_ge(
                        S["vFreePmm"], PMM[("xp", p, c, 3, DG - 1)] + 1)
                grp(item, DG, lambda i, half, st, sp, w=w: tensor.matmul(
                    pMM[0:16, half:half + TC], w[:, i * 16:(i + 1) * 16],
                    sXT[:, i * TC:(i + 1) * TC], start=st, stop=sp), pre=pre)
            elif kind == "zt":
                grp(item, DG, lambda i, half, st, sp: tensor.matmul(
                    pMM[0:1, half:half + TC], sWDc[:, i:i + 1],
                    sXT[:, i * TC:(i + 1) * TC], start=st, stop=sp))
            elif kind == "drep":
                _, p, c = item
                cn = ch(p, c)
                if cn >= 1:
                    tensor.wait_ge(S["aFreeDR"], cn)
                tensor.wait_ge(S["vDrow"], cn + 1)
                tensor.matmul(
                    pDR[:, 0:TC], sOnesR[:, :], sRowR[0:1, 0:TC],
                    start=True, stop=True,
                ).then_inc(S["pDRs"], 1)
            elif kind == "scan":
                _, p, c, n = item
                k = PMM[item]
                for gi, src in enumerate((sBmT, sCmT)):
                    kk = k + gi
                    if kk >= 3:
                        tensor.wait_ge(S["vFreePmm"], kk - 2)
                    half = mmslot(kk)
                    tensor.matmul(
                        pMM[:, half:half + TC],
                        sOneH[:, n * 128:(n + 1) * 128], src[:, :],
                        start=True, stop=True,
                    ).then_inc(S["sPmm"], 1)
            elif kind == "sx":
                _, p, c = item
                cn = ch(p, c)
                k = PMM[item]
                tensor.wait_ge(S["vFreePmm"], k - 2)
                half = mmslot(k)
                for dg in range(DG):
                    tensor.wait_ge(S["vMix"], cn * 8 + dg + 1)
                    ins = tensor.matmul(
                        pMM[0:1, half:half + TC], sOnesC[:, 0:1],
                        sYM[:, dg * TC:(dg + 1) * TC],
                        start=(dg == 0), stop=(dg == DG - 1))
                    if dg == DG - 1:
                        ins.then_inc(S["sPmm"], 1)
                    elif extra:
                        sem, val = extra.pop(0)
                        ins.then_inc(sem, val)
            elif kind == "sxsq":
                _, p, c = item
                cn = ch(p, c)
                k = PMM[item]
                tensor.wait_ge(S["vFreePmm"], k - 2)
                half = mmslot(k)
                for dg in range(DG):
                    tensor.wait_ge(S["vSq"], cn * 8 + dg + 1)
                    ins = tensor.matmul(
                        pMM[0:1, half:half + TC], sOnesC[:, 0:1],
                        tsq[dg % 2][:, 0:TC],
                        start=(dg == 0), stop=(dg == DG - 1))
                    ins.then_inc(S["sPmm" if dg == DG - 1 else "pSq"], 1)
            elif kind == "mrep":
                _, p, c = item
                grp(item, 1, lambda i, half, st, sp: tensor.matmul(
                    pMM[:, half:half + TC], sOnesR[:, :],
                    sRowR[0:1, 0:TC], start=st, stop=sp),
                    pre=lambda cn=ch(p, c): tensor.wait_ge(S["aStat"], cn + 1))
            elif kind == "irep":
                grp(item, 1, lambda i, half, st, sp: tensor.matmul(
                    pMM[:, half:half + TC], sOnesR[:, :],
                    sRowR[0:1, TC:2 * TC], start=st, stop=sp))
            elif kind == "log":
                _, p, tt = item
                pre = None
                if tt == 0:
                    def pre(p=p):
                        if p == 0:
                            tensor.wait_ge(S["sB2l"], 32)
                        tensor.wait_ge(S["sX1L"], 16 * (p + 1))
                        tensor.wait_ge(S["sX1bL"], 16 * (p + 1))
                grp(item, DG, lambda i, half, st, sp, tt=tt: tensor.matmul(
                    pMM[:, half:half + 16],
                    sX1b[:, i * TP + tt * 128:i * TP + (tt + 1) * 128],
                    sGW[:, i * 16:(i + 1) * 16], start=st, stop=sp), pre=pre)
            elif kind == "gtr":
                _, p, j = item
                grp(item, 1, lambda i, half, st, sp, j=j: tensor.transpose(
                    pMM[0:16, half:half + 128],
                    sGfin[:, j * 16:(j + 1) * 16], sEyeF[:, :]),
                    pre=lambda p=p, j=j: tensor.wait_ge(S["vGfin"], 8 * p + j + 1))
            elif kind == "b2":
                _, p, dg, sb = item
                pre = None
                if dg == 0 and sb == 0:
                    def pre(p=p):
                        if p == 0:
                            tensor.wait_ge(S["sB2l"], 16)
                        tensor.wait_ge(S["vGateT"], p + 1)
                grp(item, 1, lambda i, half, st, sp, dg=dg, sb=sb: tensor.matmul(
                    pMM[:, half:half + TC], sB2[:, dg * 128:(dg + 1) * 128],
                    sGateT[:, sb * TC:(sb + 1) * TC], start=st, stop=sp), pre=pre)
            elif kind == "grep":
                _, p, e, hf = item
                grp(item, 1, lambda i, half, st, sp, e=e, hf=hf: tensor.matmul(
                    pMM[:, half:half + TC],
                    sOneH[:, e * 128:(e + 1) * 128],
                    sGateT[:, hf * TC:(hf + 1) * TC], start=st, stop=sp))
            elif kind == "ph1":
                _, p, e, hh = item
                ge = p * E + e
                for hc in range(hh * 8, hh * 8 + 8):
                    kcol = ge * HCN + hc
                    tensor.wait_ge(S["sW1l"], 16 * (kcol + 1))
                    wslot = kcol % R1
                    for sb in range(SUB):
                        gh = (ge * HCN + hc) * SUB + sb
                        if gh >= 2:
                            tensor.wait_ge(S["aFreeH"], gh - 1)
                        for dg in range(DG):
                            ins = tensor.matmul(
                                pH[:, (gh % 2) * 512:(gh % 2) * 512 + TC],
                                sW1[:, wslot * 1024 + dg * 128:wslot * 1024 + (dg + 1) * 128],
                                sX1b[:, dg * TP + sb * TC:dg * TP + (sb + 1) * TC],
                                start=(dg == 0), stop=(dg == DG - 1))
                            if dg == DG - 1:
                                ins.then_inc(S["pHgrp"], 1)
                            elif extra:
                                sem, val = extra.pop(0)
                                ins.then_inc(sem, val)
                    extra.append((S["fW1"], 1))
            elif kind == "ph2":
                _, p, e, hh = item
                ge = p * E + e
                if hh == 0:
                    tensor.wait_ge(S["aFreeH"], 32 * (ge + 1))
                for dg in range(hh * 4, hh * 4 + 4):
                    kcol = ge * DG + dg
                    tensor.wait_ge(S["sW2l"], 16 * (kcol + 1))
                    wslot = kcol % R2
                    for sb in range(SUB):
                        gy = (ge * DG + dg) * SUB + sb
                        if gy >= 2:
                            tensor.wait_ge(S["vFreeY"], gy - 1)
                        for hc in range(HCN):
                            ins = tensor.matmul(
                                pY[:, (gy % 2) * 512:(gy % 2) * 512 + TC],
                                sW2[:, wslot * 2048 + hc * 128:wslot * 2048 + (hc + 1) * 128],
                                sH[:, hc * TP + sb * TC:hc * TP + (sb + 1) * TC],
                                start=(hc == 0), stop=(hc == HCN - 1))
                            if hc == HCN - 1:
                                ins.then_inc(S["pYgrp"], 1)
                            elif extra:
                                sem, val = extra.pop(0)
                                ins.then_inc(sem, val)
                    extra.append((S["fW2"], 1))
            elif kind == "otr":
                _, p, tg, dg = item
                pre = None
                if tg == 0 and dg == 0:
                    pre = lambda p=p: tensor.wait_ge(S["vFreeY"], 16 * E * (p + 1))
                grp(item, 1, lambda i, half, st, sp, tg=tg, dg=dg:
                    tensor.transpose(
                        pMM[:, half:half + 128],
                        sYA[:, dg * TP + tg * 128:dg * TP + (tg + 1) * 128],
                        sEyeF[:, :]), pre=pre)

    @block.vector
    def _(vector):
        vvn = [0]

        def vping(ins):
            vvn[0] += 1
            ins.then_inc(S["vv"], 1)

        def vwait():
            vector.wait_ge(S["vv"], vvn[0])

        def half_of(key):
            return mmslot(PMM[key])

        vector.wait_ge(S["sConst"], 16 * NCONST)
        vector.memset(sDum[:, :], 0.0)
        vector.reciprocal(sInvA[:, :], sAnat[:, :])

        defq = []

        def flush(kmax=99):
            while defq and kmax > 0:
                defq.pop(0)()
                kmax -= 1

        for item in sch:
            kind = item[0]
            if kind == "ph1":
                flush(8)
                continue
            if kind == "b1i":
                j = item[1]
                vector.wait_ge(S["sPmm"], PMM[item] + 1)
                h = half_of(item)
                vector.tensor_copy(
                    sB1c[:, j * 16:(j + 1) * 16], pMM[:, h:h + 16],
                ).then_inc(S["vFreePmm"], 1)
            elif kind == "wbi":
                j = item[1]
                vector.wait_ge(S["sPmm"], PMM[item] + 1)
                h = half_of(item)
                dst = sWBc if j < 8 else sWCc
                dgj = j % 8
                vector.tensor_copy(
                    dst[:, dgj * 16:(dgj + 1) * 16], pMM_r[:, h:h + 16],
                ).then_inc(S["vFreePmm"], 1)
            elif kind == "xp":
                _, p, c, q, dg = item
                if q == 0 and dg == 0 and ch(p, c) > 0:
                    prev = (p, c - 1) if c > 0 else (p - 1, NCH - 1)
                    vector.wait_ge(S["sPmm"], PMM[("zt",) + prev] + 1)
                vector.wait_ge(S["sPmm"], PMM[item] + 1)
                h = half_of(item)
                vector.tensor_copy(
                    sXT[:, dg * TC + q * 128:dg * TC + (q + 1) * 128],
                    pMM_r[:, h:h + 128],
                ).then_inc(S["vFreePmm"], 1)
            elif kind in ("bmt", "cmt"):
                _, p, c = item
                vector.wait_ge(S["sPmm"], PMM[item] + 1)
                h = half_of(item)
                dst, bias = (sBmT, sBB) if kind == "bmt" else (sCmT, sBC)
                vector.tensor_scalar(
                    dst[:, :], pMM[0:16, h:h + TC], bias[:, 0:1], None, ALU.add,
                ).then_inc(S["vFreePmm"], 1)
            elif kind == "zt":
                _, p, c = item
                cn = ch(p, c)
                vector.wait_ge(S["sPmm"], PMM[item] + 1)
                h = half_of(item)
                vector.tensor_scalar(
                    sRowF[0:1, 0:TC], pMM[0:1, h:h + TC],
                    sBD[0:1, 0:1], sBD[0:1, 1:2], ALU.add, ALU.add,
                ).then_inc(S["vFreePmm"], 1)
                vector.tensor_copy(sDum[0:1, 0:1], sDum[0:1, 1:2]
                                   ).then_inc(S["vZrow"], 1)
                vector.wait_ge(S["aSp"], cn + 1)
                vector.tensor_scalar_max(
                    sRowF[0:1, 0:TC], sRowF[0:1, 0:TC], 0.0)
                vector.tensor_add(
                    sRowR[0:1, 0:TC], sRowF[0:1, 0:TC], sRowF[0:1, TC:2 * TC],
                ).then_inc(S["vDrow"], 1)
                if cn % 4 == 0:
                    vping(vector.memset(sCarry[:, :], 0.0))
                    vwait()
            elif kind == "scan":
                _, p, c, n = item
                cn = ch(p, c)
                abase = cn * 128
                k = PMM[item]
                hb = mmslot(k)
                hcr = mmslot(k + 1)
                flush()
                vector.wait_ge(S["sPmm"], k + 2)
                par = n % 2
                vector.tensor_copy(sBrep[:, par * TC:(par + 1) * TC],
                                   pMM[:, hb:hb + TC],
                                   ).then_inc(S["vFreePmm"], 1)
                vector.tensor_copy(sCrep[:, par * TC:(par + 1) * TC],
                                   pMM[:, hcr:hcr + TC],
                                   ).then_inc(S["vFreePmm"], 1)

                def dgstep(dg, p=p, c=c, n=n, cn=cn, abase=abase, par=par):
                    if n == 0 and dg == 0 and cn >= 1:
                        vector.wait_ge(S["x1s"], 16 * cn)
                    gl = n * 8 + dg
                    col = dg * 16 + n
                    vector.wait_ge(S["aAbar"], abase + gl + 1)
                    vector.scalar_tensor_tensor(
                        t1w[:, :], sAbar[:, (gl % 4) * TC:(gl % 4 + 1) * TC],
                        1.0, sXT[:, dg * TC:(dg + 1) * TC].bitcast(F32),
                        ALU.subtract, ALU.mult)
                    vector.scalar_tensor_tensor(
                        t2w[:, :], t1[:, :], sInvA[:, col:col + 1],
                        sBrep[:, par * TC:(par + 1) * TC],
                        ALU.mult, ALU.mult)
                    vector.tensor_tensor_scan(
                        sT3[:, :], sAbar[:, (gl % 4) * TC:(gl % 4 + 1) * TC],
                        t2[:, :], sCarry[:, gl:gl + 1], ALU.mult, ALU.add)
                    vector.tensor_tensor(
                        t2w[:, :], sT3[:, :],
                        sCrep[:, par * TC:(par + 1) * TC], ALU.mult)
                    dst = sYM[:, dg * TC:(dg + 1) * TC]
                    if n == 0:
                        ins = vector.tensor_add(
                            dst, t2[:, :], sXT[:, dg * TC:(dg + 1) * TC].bitcast(F32))
                    else:
                        ins = vector.tensor_add(dst, dst.bitcast(F32), t2[:, :])
                        if n == NST - 1:
                            ins.then_inc(S["vMix"], 1)
                    vector.tensor_copy(
                        sCarry[:, gl:gl + 1], sT3[:, TC - 1:TC],
                    ).then_inc(S["vAbar"], 1)
                for dg in range(DG):
                    defq.append(lambda dg=dg, f=dgstep: f(dg))
            elif kind == "sx":
                _, p, c = item
                cn = ch(p, c)
                flush()
                for dg in range(DG):
                    if dg >= 2:
                        vector.wait_ge(S["pSq"], cn * 7 + dg - 1)
                    src = sYM[:, dg * TC:(dg + 1) * TC].bitcast(F32)
                    vector.tensor_tensor(
                        tsq[dg % 2][:, 0:TC], src, src, ALU.mult,
                    ).then_inc(S["vSq"], 1)
            elif kind == "sxsq":
                _, p, c = item
                vector.wait_ge(S["sPmm"], PMM[item] + 1)
                hsx = half_of(("sx", p, c))
                hsq = half_of(item)
                vector.tensor_scalar_mul(
                    sRowR[0:1, 0:TC], pMM[0:1, hsx:hsx + TC], 1.0 / D,
                ).then_inc(S["vFreePmm"], 1)
                vector.tensor_tensor(
                    sRowF[0:1, TC:2 * TC], sRowR[0:1, 0:TC].bitcast(F32),
                    sRowR[0:1, 0:TC].bitcast(F32), ALU.mult)
                vector.tensor_scalar_mul(
                    sRowF[0:1, 2 * TC:3 * TC], pMM[0:1, hsq:hsq + TC], 1.0 / D,
                ).then_inc(S["vFreePmm"], 1)
                vector.tensor_sub(
                    sRowF[0:1, 2 * TC:3 * TC], sRowF[0:1, 2 * TC:3 * TC],
                    sRowF[0:1, TC:2 * TC])
                vector.tensor_scalar_add(
                    sRowF[0:1, 2 * TC:3 * TC], sRowF[0:1, 2 * TC:3 * TC], LNEPS,
                ).then_inc(S["vStat"], 1)
            elif kind == "irep":
                _, p, c = item
                vector.wait_ge(S["sPmm"], PMM[item] + 1)
                hm = half_of(("mrep", p, c))
                hi = half_of(item)
                for dg in range(DG):
                    dst = sYM[:, dg * TC:(dg + 1) * TC]
                    ins = vector.tensor_tensor(
                        t1w[:, :], dst.bitcast(F32), pMM[:, hm:hm + TC],
                        ALU.subtract)
                    if dg == DG - 1:
                        ins.then_inc(S["vFreePmm"], 1)
                    ins = vector.tensor_tensor(
                        dst, t1[:, :], pMM[:, hi:hi + TC], ALU.mult)
                    if dg == DG - 1:
                        ins.then_inc(S["vFreePmm"], 1)
                vector.tensor_copy(sDum[0:1, 0:1], sDum[0:1, 1:2]
                                   ).then_inc(S["vLn1"], 1)
            elif kind == "log":
                _, p, tt = item
                vector.wait_ge(S["sPmm"], PMM[item] + 1)
                hl = half_of(item)
                vector.tensor_copy(
                    sGw1[:, tt * 16:(tt + 1) * 16], pMM[:, hl:hl + 16],
                ).then_inc(S["vFreePmm"], 1)
                if tt == 7:
                    g1 = [sGw1[:, t * 16:(t + 1) * 16] for t in range(8)]
                    g2 = [sGw2[:, t * 16:(t + 1) * 16] for t in range(8)]
                    g3 = [sGw3[:, t * 16:(t + 1) * 16] for t in range(8)]

                    def col(s):
                        return [sGcol[:, s * 8 + t:s * 8 + t + 1]
                                for t in range(8)]
                    for t in range(8):
                        vector.reduce_max(col(0)[t], g1[t], axis=AX.X)
                    for t in range(8):
                        ins = vector.tensor_scalar(
                            g3[t], g1[t], col(0)[t], None, ALU.subtract)
                    ins.then_inc(S["vGexp"], 1)
                    vector.wait_ge(S["aGexp"], p + 1)
                    for t in range(8):
                        vector.reduce_sum(col(1)[t], g2[t], axis=AX.X)
                    for t in range(8):
                        vector.reciprocal(col(2)[t], col(1)[t])
                    for t in range(8):
                        vector.tensor_scalar(
                            g1[t], g2[t], col(2)[t], None, ALU.mult)
                    for t in range(8):
                        vector.reduce_max(col(3)[t], g1[t], axis=AX.X)
                    for t in range(8):
                        vector.tensor_scalar(
                            g2[t], g1[t], col(3)[t], None, ALU.is_equal)
                    for t in range(8):
                        vector.scalar_tensor_tensor(
                            g3[t], g2[t], -2.0, g1[t], ALU.mult, ALU.add)
                    for t in range(8):
                        vector.reduce_max(col(4)[t], g3[t], axis=AX.X)
                    for t in range(8):
                        vector.tensor_scalar(
                            g2[t], g3[t], col(4)[t], None, ALU.is_equal)
                    for t in range(8):
                        vector.scalar_tensor_tensor(
                            g3[t], g2[t], -2.0, g3[t], ALU.mult, ALU.add)
                    for t in range(8):
                        vector.reduce_max(col(5)[t], g3[t], axis=AX.X)
                    for t in range(8):
                        vector.tensor_scalar(
                            g2[t], g1[t], col(5)[t], None, ALU.is_ge)
                    for t in range(8):
                        vector.tensor_mul(g1[t], g1[t], g2[t])
                    for t in range(8):
                        vector.reduce_sum(col(6)[t], g1[t], axis=AX.X)
                    for t in range(8):
                        vector.tensor_scalar_max(col(6)[t], col(6)[t], GEPS)
                    for t in range(8):
                        vector.reciprocal(col(7)[t], col(6)[t])
                    for t in range(8):
                        vector.tensor_scalar(
                            sGfin[:, t * 16:(t + 1) * 16], g1[t],
                            col(7)[t], None, ALU.mult,
                        ).then_inc(S["vGfin"], 1)
            elif kind == "gtr":
                _, p, j = item
                vector.wait_ge(S["sPmm"], PMM[item] + 1)
                h = half_of(item)
                vector.tensor_copy(
                    sGateT[:, j * 128:(j + 1) * 128], pMM[0:16, h:h + 128],
                ).then_inc(S["vFreePmm"], 1)
                if j == 7:
                    vector.tensor_copy(sDum[0:1, 0:1], sDum[0:1, 1:2]
                                       ).then_inc(S["vGateT"], 1)
            elif kind == "b2":
                _, p, dg, sb = item
                if dg == 0 and sb == 0:
                    vector.wait_ge(S["sX1L"], 16 * (p + 1))
                vector.wait_ge(S["sPmm"], PMM[item] + 1)
                hb2 = half_of(item)
                vector.tensor_add(
                    sYA[:, dg * TP + sb * TC:dg * TP + (sb + 1) * TC],
                    pMM[:, hb2:hb2 + TC],
                    shx1[:, dg * TP + sb * TC:dg * TP + (sb + 1) * TC],
                ).then_inc(S["vFreePmm"], 1)
            elif kind == "grep":
                _, p, e, hf = item
                vector.wait_ge(S["sPmm"], PMM[item] + 1)
                hg = half_of(item)
                vector.tensor_copy(
                    sGRep[:, hf * TC:(hf + 1) * TC], pMM[:, hg:hg + TC],
                ).then_inc(S["vFreePmm"], 1)
            elif kind == "ph2":
                _, p, e, hh = item
                ge = p * E + e
                for dg in range(hh * 4, hh * 4 + 4):
                    for sb in range(SUB):
                        gy = (ge * DG + dg) * SUB + sb
                        if sb == 0 and defq:
                            defq.pop(0)()
                        vector.wait_ge(S["pYgrp"], gy + 1)
                        vector.tensor_tensor(
                            t1w[:, :], pY[:, (gy % 2) * 512:(gy % 2) * 512 + TC],
                            sGRep[:, sb * TC:(sb + 1) * TC], ALU.mult)
                        dst = sYA[:, dg * TP + sb * TC:dg * TP + (sb + 1) * TC]
                        vector.tensor_add(dst, dst, t1[:, :],
                                          ).then_inc(S["vFreeY"], 1)
            elif kind == "otr":
                _, p, tg, dg = item
                g = p * 8 + tg
                ob = (g % 2) * D
                if dg == 0 and g >= 2:
                    flush()
                    vector.wait_ge(S["st"], 16 * (g - 1))
                vector.wait_ge(S["sPmm"], PMM[item] + 1)
                h = half_of(item)
                vector.tensor_copy(
                    sOutT[:, ob + dg * 128:ob + (dg + 1) * 128],
                    pMM[:, h:h + 128],
                ).then_inc(S["vFreePmm"], 1)
            elif kind == "ln2":
                _, p, tg = item
                g = p * 8 + tg
                ob = (g % 2) * D

                def ln2chain(g=g, ob=ob):
                    oT = sOutT[:, ob:ob + D]
                    vector.tensor_tensor(
                        t1w[:, :], oT[:, 0:TC], oT[:, 0:TC], ALU.mult)
                    vector.tensor_tensor(
                        t2w[:, :], oT[:, TC:2 * TC], oT[:, TC:2 * TC], ALU.mult)
                    vector.reduce_sum(sL2c[:, 0:1], oT[:, 0:TC], axis=AX.X)
                    vping(vector.reduce_sum(sL2c[:, 1:2], oT[:, TC:2 * TC], axis=AX.X))
                    vwait()
                    vector.tensor_add(sL2c[:, 0:1], sL2c[:, 0:1], sL2c[:, 1:2])
                    vector.reduce_sum(sL2c[:, 2:3], t1[:, :], axis=AX.X)
                    vping(vector.reduce_sum(sL2c[:, 3:4], t2[:, :], axis=AX.X))
                    vwait()
                    vping(vector.tensor_add(sL2c[:, 2:3], sL2c[:, 2:3], sL2c[:, 3:4]))
                    vwait()
                    vping(vector.tensor_scalar_mul(sL2c[:, 4:5], sL2c[:, 0:1], 1.0 / D))
                    vwait()
                    vping(vector.tensor_mul(sL2c[:, 5:6], sL2c[:, 4:5], sL2c[:, 4:5]))
                    vwait()
                    vping(vector.tensor_scalar_mul(sL2c[:, 6:7], sL2c[:, 2:3], 1.0 / D))
                    vwait()
                    vping(vector.tensor_sub(sL2c[:, 6:7], sL2c[:, 6:7], sL2c[:, 5:6]))
                    vwait()
                    vector.tensor_scalar_add(
                        sL2c[:, 6:7], sL2c[:, 6:7], LNEPS,
                    ).then_inc(S["vLn2"], 1)
                    vector.wait_ge(S["aLn2"], g + 1)
                    vector.tensor_scalar(
                        oT, oT, sL2c[:, 4:5], sL2c[:, 7:8],
                        ALU.subtract, ALU.mult,
                    ).then_inc(S["vOut"], 1)
                ln2chain()

    @block.scalar
    def _(scalar):
        adefq = []

        def aflush(kmax=99):
            while adefq and kmax > 0:
                adefq.pop(0)()
                kmax -= 1

        scalar.wait_ge(S["sConst"], 16 * NCONST)
        for item in sch:
            kind = item[0]
            if kind == "xin":
                _, p, c, q = item
                aflush()
                Q = ch(p, c) * 4 + q
                if Q > 0:
                    pq = Q - 1
                    key = ("xp", pq // 8, (pq // 4) % 2, pq % 4, DG - 1)
                    scalar.wait_ge(S["sPmm"], PMM[key] + 1)
                row = p * TP + c * TC + q * 128
                scalar.dma_start(
                    out=sInX[:, :], in_=x_d[row:row + 128, :]
                ).then_inc(S["sXq"], 16)
            elif kind == "zt":
                _, p, c = item
                cn = ch(p, c)
                aflush()
                scalar.wait_ge(S["vZrow"], cn + 1)
                scalar.activation(
                    sRowF[0:1, TC:2 * TC], sRowF[0:1, 0:TC], AF.Abs)
                scalar.activation(
                    sRowF[0:1, 2 * TC:3 * TC], sRowF[0:1, TC:2 * TC],
                    AF.Exp, scale=-1.0)
                scalar.activation(
                    sRowF[0:1, TC:2 * TC], sRowF[0:1, 2 * TC:3 * TC],
                    AF.Ln, bias=1.0,
                ).then_inc(S["aSp"], 1)
            elif kind == "scan":
                _, p, c, n = item
                cn = ch(p, c)
                abase = cn * 128
                aflush()
                if n == 0:
                    scalar.wait_ge(S["pDRs"], cn + 1)
                for dg in range(DG):
                    gl = n * 8 + dg
                    col = dg * 16 + n
                    if gl >= 4:
                        scalar.wait_ge(S["vAbar"], abase + gl - 3)
                    scalar.activation(
                        sAbar[:, (gl % 4) * TC:(gl % 4 + 1) * TC],
                        pDR[:, 0:TC], AF.Exp, scale=sAnat[:, col:col + 1],
                    ).then_inc(S["aAbar"], 1)
                if n == NST - 1:
                    scalar.copy(sDum[0:1, 2:3], sDum[0:1, 3:4]
                                ).then_inc(S["aFreeDR"], 1)
            elif kind == "sxsq":
                _, p, c = item
                cn = ch(p, c)
                aflush()
                scalar.wait_ge(S["vStat"], cn + 1)
                scalar.activation(
                    sRowF[0:1, TC:2 * TC], sRowF[0:1, 2 * TC:3 * TC], AF.Ln)
                scalar.activation(
                    sRowR[0:1, TC:2 * TC],
                    sRowF[0:1, TC:2 * TC], AF.Exp, scale=-0.5,
                ).then_inc(S["aStat"], 1)
            elif kind == "x1st":
                _, p, c = item
                cn = ch(p, c)
                scalar.wait_ge(S["vLn1"], cn + 1)
                scalar.dma_start(
                    out=x1scr_d[p * 128:(p + 1) * 128, :].rearrange(
                        "p (dg t) -> p dg t", dg=DG)[:, :, c * TC:(c + 1) * TC],
                    in_=sYM[:, :].rearrange("p (dg t) -> p dg t", dg=DG),
                ).then_inc(S["x1s"], 16)
            elif kind == "log":
                _, p, tt = item
                aflush()
                if tt == 7:
                    scalar.wait_ge(S["vGexp"], p + 1)
                    scalar.activation(
                        sGw2[:, :], sGw3[:, :], AF.Exp,
                    ).then_inc(S["aGexp"], 1)
            elif kind == "ph1":
                _, p, e, hh = item
                ge = p * E + e
                for hc in range(hh * 8, hh * 8 + 8):
                    if hc % 4 == 2:
                        aflush(1)
                    for sb in range(SUB):
                        gh = (ge * HCN + hc) * SUB + sb
                        if e == 0 and hc == 0 and sb == 0:
                            scalar.wait_ge(
                                S["vFreePmm"],
                                PMM[("b2", p, DG - 1, SUB - 1)] + 1)
                        if hc == 0 and sb == 0:
                            scalar.wait_ge(S["pYgrp"], 16 * ge)
                        scalar.wait_ge(S["pHgrp"], gh + 1)
                        scalar.activation(
                            sH[:, hc * TP + sb * TC:hc * TP + (sb + 1) * TC],
                            pH[:, (gh % 2) * 512:(gh % 2) * 512 + TC],
                            AF.Relu, bias=sB1c[:, hc * 16 + e:hc * 16 + e + 1],
                        ).then_inc(S["aFreeH"], 1)
            elif kind == "ln2":
                _, p, tg = item
                g = p * 8 + tg

                def achain(g=g):
                    scalar.wait_ge(S["vLn2"], g + 1)
                    scalar.activation(sL2c[:, 6:7], sL2c[:, 6:7], AF.Ln)
                    scalar.activation(
                        sL2c[:, 7:8], sL2c[:, 6:7], AF.Exp, scale=-0.5,
                    ).then_inc(S["aLn2"], 1)
                achain()

    ctx.close()
    return nc


def _consts():
    eye = np.eye(128, dtype=np.float32)
    onescol = np.ones((128, 1), dtype=np.float32)
    onesrow = np.ones((1, 128), dtype=np.float32)
    onehot = np.zeros((E, E * 128), dtype=np.float32)
    for e in range(E):
        onehot[e, e * 128:(e + 1) * 128] = 1.0
    import ml_dtypes
    return {
        "eyer": eye, "eyef": eye, "onescol": onescol, "onesrow": onesrow,
        "onehot": onehot.astype(ml_dtypes.bfloat16),
    }


def make_in_map(core, inputs):
    x = inputs["L0_attn_sequence"]
    f = np.float32
    m = {
        "x": np.ascontiguousarray(
            x[core * BPC:(core + 1) * BPC].reshape(TOK, D).astype(f)),
        "amat": np.ascontiguousarray(inputs["L0_attn_A"].astype(f)),
        "wb": np.ascontiguousarray(inputs["L0_attn_W_B"].astype(f)),
        "bb": np.ascontiguousarray(
            inputs["L0_attn_b_B"].astype(f).reshape(NST, 1)),
        "wc": np.ascontiguousarray(inputs["L0_attn_W_C"].astype(f)),
        "bc": np.ascontiguousarray(
            inputs["L0_attn_b_C"].astype(f).reshape(NST, 1)),
        "wd": np.ascontiguousarray(
            inputs["L0_attn_W_Delta"].astype(f).reshape(D, 1)),
        "bd": np.ascontiguousarray(
            inputs["L0_attn_b_Delta"].astype(f).reshape(1, 1)),
        "biasd": np.ascontiguousarray(
            inputs["L0_attn_bias_Delta"].astype(f).reshape(1, 1)),
        "gw": np.ascontiguousarray(inputs["L0_ffn_gate"].astype(f)),
        "w1": np.ascontiguousarray(inputs["L0_ffn_W1"].astype(f)),
        "b1": np.ascontiguousarray(inputs["L0_ffn_b1"].astype(f)),
        "w2": np.ascontiguousarray(inputs["L0_ffn_W2"].astype(f)),
        "b2": np.ascontiguousarray(inputs["L0_ffn_b2"].astype(f)),
    }
    m.update(_consts())
    return m


def kernel(**inputs):
    nc = build_program(4)
    in_maps = [make_in_map(core, inputs) for core in range(NCORES)]
    res = run_bass_kernel_spmd(nc, in_maps, core_ids=list(range(NCORES)))
    out = np.concatenate(
        [r["y"].reshape(BPC, T, D) for r in res.results], axis=0)
    return out.astype(np.float32)
